# version 08
# baseline (speedup 1.0000x reference)
import numpy as np
from contextlib import ExitStack

import concourse.bass as bass
import concourse.mybir as mybir
from concourse.bass_utils import run_bass_kernel_spmd

F32 = mybir.dt.float32
BF16 = mybir.dt.bfloat16
U32 = mybir.dt.uint32
AF = mybir.ActivationFunctionType
OP = mybir.AluOpType
AX = mybir.AxisListType
P = 128

DM = 2048
LSEQ = 4096
BSPLIT = 1280
NHL = 16
DEPTH = 4
TLAG = 7
ALAG = 9
EPS = 1e-5

MASK_OFFS = [0, 512, 1024, 1536, 2048, 2048 + 256]
MASK_TOTAL = 4 * 512 + 2 * 256

SEM_NAMES = [
    "c_ld", "s_rdy", "p0w", "p0x0", "p0x1", "p0g", "p0evA", "p0evD", "ph0",
    "s_z", "s_sg", "s_act", "s_scan", "s_mult", "s_tr", "s_ev", "s_av",
    "s_oev", "s_od0", "s_od1", "ph1", "s_flg", "s_xch", "p3ld", "p3w", "s_evA", "s_evD",
    "p3mm", "p3r", "p3lnA", "p3lnB", "p3ln", "p3tr", "p3te", "p3g", "p3gv",
    "p3gd", "p3gx", "xrld0", "xrld1", "x1w", "ph3", "p4w", "p4g", "p4ev",
    "ph4a", "p4bg", "p4bev", "x1ld0", "x1ld1", "r2w0", "r2w1", "ph4b",
    "r2ld0", "r2ld1", "p5a", "p5b", "p5ln", "s_out0", "s_out1", "s_sgn", "s_dbg", "smv", "sma",
]


class Unit:
    __slots__ = ("h", "tl", "kb", "w", "sweep", "sweep_start", "sweep_end",
                 "mvar", "zm_count", "prev_same_zpar", "idx")


def body_cfg(ut):
    if ut == 0:
        ntl, kvlen, qlen = 10, 4096, 1280

        def blocks(tl):
            return [(kb, 512) for kb in range(tl // 4, 8)]

        def mvar(tl):
            return tl % 4
    else:
        ntl, kvlen, qlen = 22, 2816, 2816

        def blocks(tl):
            kb0 = tl // 4
            bl = [(kb, 512) for kb in range(kb0, 5)]
            bl.append((5, 256))
            return bl

        def mvar(tl):
            return 4 + (tl - 20) if tl >= 20 else tl % 4
    return dict(ut=ut, ntl=ntl, kvlen=kvlen, qlen=qlen, blocks=blocks,
                mvar=mvar, ttok=qlen // 2, itout=(qlen // 2) // 128)


class Pair:
    __slots__ = ("h", "tl", "kb", "start", "nh", "w", "w0", "sweep",
                 "sweep_start", "mvar", "zm_count", "prev_same_zpar",
                 "lasthalf", "idx")


def make_units(cfg):
    units = []
    sweep = 0
    for h in range(NHL):
        for tl in range(cfg["ntl"]):
            bl = cfg["blocks"](tl)
            for bi, (kb, w) in enumerate(bl):
                u = Unit()
                u.h, u.tl, u.kb, u.w = h, tl, kb, w
                u.sweep = sweep
                u.sweep_start = bi == 0
                u.sweep_end = bi == len(bl) - 1
                u.mvar = cfg["mvar"](tl) if bi == 0 else None
                units.append(u)
            sweep += 1
    for i, u in enumerate(units):
        u.idx = i
        u.zm_count = None
    pairs = []
    pair_of = [None] * len(units)
    i = 0
    while i < len(units):
        u = units[i]
        nh = 2 if (i + 1 < len(units) and units[i + 1].sweep == u.sweep) else 1
        p = Pair()
        p.h, p.tl, p.kb = u.h, u.tl, u.kb
        p.start = i
        p.nh = nh
        p.w0 = u.w
        p.w = u.w + (units[i + 1].w if nh == 2 else 0)
        p.sweep = u.sweep
        p.sweep_start = u.sweep_start
        p.mvar = u.mvar
        p.lasthalf = i + nh - 1
        p.idx = len(pairs)
        pair_of[i] = p.idx
        if nh == 2:
            pair_of[i + 1] = p.idx
        pairs.append(p)
        i += nh
    zmc = 0
    last = {0: None, 1: None}
    zcumP = []
    for p in pairs:
        if p.mvar is not None:
            zp = zmc % 2
            p.prev_same_zpar = last[zp]
            last[zp] = p.idx
            zmc += 1
            p.zm_count = zmc
        else:
            p.zm_count = None
            p.prev_same_zpar = None
        zcumP.append(zmc)
    return units, (pairs, pair_of, zcumP)


def p0_blocks(cfg, which):
    ln = cfg["kvlen"] if which in ("k", "v") else cfg["qlen"]
    out = []
    col = 0
    while col < ln:
        w = min(256, ln - col)
        out.append((col, w))
        col += w
    return out


def p0_schedule(cfg):
    groups = []
    blocks = []
    nblk = 0
    for which in ("k", "v", "q"):
        for (col, w) in p0_blocks(cfg, which):
            blocks.append((which, nblk, col, w))
            if which == "v":
                for ch in range(w // 128):
                    for dvb in range(2):
                        groups.append(dict(p=which, nblk=nblk, col=col, w=w,
                                           sub=(ch, dvb), ev="D"))
            else:
                for dt_ in range(8):
                    groups.append(dict(p=which, nblk=nblk, col=col, w=w,
                                       sub=dt_, ev="A"))
            nblk += 1
    ca = cd = 0
    blk_end = {}
    for gi, g in enumerate(groups):
        if g["ev"] == "A":
            ca += 1
        else:
            cd += 1
        g["cumA"], g["cumD"] = ca, cd
        blk_end[g["nblk"]] = gi + 1
    widx = {"k": 1, "v": 2, "q": 3}
    for g in groups:
        g["widx"] = widx[g["p"]]
    return groups, blocks, blk_end


def build_masks():
    m = np.zeros((P, MASK_TOTAL), np.float32)
    ii = np.arange(P)[:, None]
    for q in range(4):
        jj = np.arange(512)[None, :]
        m[:, q * 512:(q + 1) * 512] = np.where(jj <= 128 * q + ii, -1e9, 0.0)
    for q in range(2):
        jj = np.arange(256)[None, :]
        off = 2048 + q * 256
        m[:, off:off + 256] = np.where(jj <= 128 * q + ii, -1e9, 0.0)
    return m


def rearr(ap):
    return ap.rearrange("(a p) n -> p a n", p=128)



def emit_tensor(t, pl, d, tn, S):
    cfg = pl["cfg"]
    t.wait_ge(S["s_rdy"], 1)

    groups, blocks, blk_end = pl["p0"]
    pp0, W, xTb = pl["pp0"], pl["Wp"], pl["xTb"]
    for gi, g in enumerate(groups):
        t.wait_ge(S["p0w"], 16 * g["widx"])
        par = g["nblk"] % 2
        t.wait_ge(S["p0x0"] if par == 0 else S["p0x1"], 16 * (g["nblk"] // 2 + 1))
        if gi >= 4:
            gp = groups[gi - 4]
            if gp["cumA"]:
                t.wait_ge(S["p0evA"], gp["cumA"])
            if gp["cumD"]:
                t.wait_ge(S["p0evD"], gp["cumD"])
        bank = gi % 4
        if g["p"] in ("k", "q"):
            dt_ = g["sub"]
            for dmc in range(16):
                mm = t.matmul(pp0[:, bank, 0:g["w"]],
                              W[:, dmc, dt_ * 128:(dt_ + 1) * 128],
                              xTb[:, par, dmc, 0:g["w"]],
                              start=(dmc == 0), stop=(dmc == 15))
        else:
            ch, dvb = g["sub"]
            for dmc in range(16):
                mm = t.matmul(pp0[:, bank, 0:512],
                              xTb[:, par, dmc, ch * 128:(ch + 1) * 128],
                              W[:, dmc, dvb * 512:(dvb + 1) * 512],
                              start=(dmc == 0), stop=(dmc == 15))
        mm.then_inc(S["p0g"], 1)
    t.sem_inc(S["ph0"], 1)

    t.wait_ge(S["ph0"], 4)
    totA = pl["p0"][0][-1]["cumA"]
    totD = pl["p0"][0][-1]["cumD"]
    t.wait_ge(S["p0evA"], totA)
    t.wait_ge(S["p0evD"], totD)
    units = pl["units"]
    pairs, pair_of, zcumP = pl["zcum"]
    QT, KT, V = pl["QT"], pl["KT"], pl["V"]
    pz, pt, po = pl["pz"], pl["pt"], pl["po"]
    att, attT = pl["att"], pl["attT"]
    n = len(units)
    for i in range(n + ALAG):
        if i < n and pair_of[i] is not None and pairs[pair_of[i]].start == i:
            p = pairs[pair_of[i]]
            pi = p.idx
            if pi >= 2:
                t.wait_ge(S["s_sg"], pi - 1)
            p0h = (p.h % 2) * 64
            for hh in range(p.nh):
                u = units[i + hh]
                qs = QT[p0h:p0h + 64, p.h // 2, p.tl * 128:(p.tl + 1) * 128]
                ks = KT[p0h:p0h + 64, p.h // 2, u.kb * 512:u.kb * 512 + u.w]
                masked = hh == 0 and p.mvar is not None
                mm = t.matmul(pz[:, (pi % 2) * 2 + hh, 0:u.w], qs, ks,
                              start=True, stop=not masked)
                if masked:
                    off = MASK_OFFS[p.mvar]
                    mm = t.matmul(pz[:, (pi % 2) * 2 + hh, 0:u.w], tn["ident"][:],
                                  tn["masks"][:, off:off + u.w],
                                  start=False, stop=True)
            mm.then_inc(S["s_z"], 1)
        j = i - TLAG
        if 0 <= j < n:
            u = units[j]
            p = pairs[pair_of[j]]
            b2 = j % 2
            t.wait_ge(S["s_mult"], p.idx + 1)
            if j >= 2:
                t.wait_ge(S["s_evA"] if j % 2 == 0 else S["s_evD"], j // 2)
            coff = (j - p.start) * 512
            for ch in range(u.w // 128):
                mm = t.matmul(pt[:, b2, ch * 128:(ch + 1) * 128],
                              att[:, p.idx % 3, coff + ch * 128:coff + (ch + 1) * 128],
                              tn["ident"][:], start=True, stop=True)
            mm.then_inc(S["s_tr"], 1)
        j = i - ALAG
        if 0 <= j < n:
            u = units[j]
            b2, sp = j % 2, units[j].sweep % 2
            t.wait_ge(S["s_evA"] if j % 2 == 0 else S["s_evD"], j // 2 + 1)
            if u.sweep_start and u.sweep >= 2:
                t.wait_ge(S["s_oev"], u.sweep - 1)
            nch = u.w // 128
            for ch in range(nch):
                tt_ = u.kb * 4 + ch
                mm = t.matmul(po[0:64, sp, 0:128],
                              V[:, tt_, u.h * 64:(u.h + 1) * 64],
                              attT[:, b2, ch * 128:(ch + 1) * 128],
                              start=(u.sweep_start and ch == 0),
                              stop=(u.sweep_end and ch == nch - 1))
            mm.then_inc(S["s_av"], 1)
    t.sem_inc(S["ph1"], 1)

    t.wait_ge(S["s_xch"], 1)
    n_units = len(pl["units"])
    npairs = len(pl["zcum"][0])
    nsweeps = pl["units"][-1].sweep + 1
    t.wait_ge(S["s_sg"], npairs)
    t.wait_ge(S["s_act"], npairs)
    t.wait_ge(S["s_scan"], npairs)
    t.wait_ge(S["s_evA"], (n_units + 1) // 2)
    t.wait_ge(S["s_evD"], n_units // 2)
    t.wait_ge(S["s_oev"], nsweeps)
    t.wait_ge(S["p3w"], 16 * 2)
    t.wait_ge(S["p3ld"], 16 * 16)
    itout = cfg["itout"]
    oTa, WoT, x1T, x1b = pl["oTa"], pl["WoT"], pl["x1T"], pl["x1b"]
    pp3, pg = pl["pp3"], pl["pg"]
    for it in range(itout):
        if it >= 1:
            t.wait_ge(S["p3te"], 4 * it)
        for mc in range(4):
            for dvc in range(16):
                mm = t.matmul(pp3[:, mc, :], oTa[:, dvc, it * 128:(it + 1) * 128],
                              WoT[:, dvc, mc * 512:(mc + 1) * 512],
                              start=(dvc == 0), stop=(dvc == 15))
            mm.then_inc(S["p3mm"], 1)
        t.wait_ge(S["p3ln"], it + 1)
        t.wait_ge(S["p3r"], 4 * (it + 1))
        for ch in range(16):
            mm = t.matmul(pp3[:, ch // 4, (ch % 4) * 128:(ch % 4 + 1) * 128],
                          x1b[:, ch * 128:(ch + 1) * 128], tn["ident"][:],
                          start=True, stop=True)
        mm.then_inc(S["p3tr"], 1)
        t.wait_ge(S["p3te"], 4 * (it + 1))
        if it >= 1:
            t.wait_ge(S["p3gv"], it)
        for dmc in range(16):
            mm = t.matmul(pg[:, 0:8], x1T[:, dmc, it * 128:(it + 1) * 128],
                          pl["WgT"][:, dmc, :], start=(dmc == 0), stop=(dmc == 15))
        mm.then_inc(S["p3g"], 1)
    t.sem_inc(S["ph3"], 1)

    t.wait_ge(S["ph3"], 4)
    t.wait_ge(S["p4w"], 16 * 1)
    pp4, hT, W4 = pl["pp4"], pl["hT"], pl["W4"]
    tokbs = [(c, min(512, cfg["ttok"] - c)) for c in range(0, cfg["ttok"], 512)]
    g4 = 0
    for dupt in range(16):
        for (tc, tw) in tokbs:
            if g4 >= 2:
                t.wait_ge(S["p4ev"], g4 - 1)
            for dmc in range(16):
                mm = t.matmul(pp4[:, g4 % 2, 0:tw],
                              W4[:, dmc, dupt * 128:(dupt + 1) * 128],
                              x1T[:, dmc, tc:tc + tw],
                              start=(dmc == 0), stop=(dmc == 15))
            mm.then_inc(S["p4g"], 1)
            g4 += 1
    t.sem_inc(S["ph4a"], 1)

    t.wait_ge(S["ph4a"], 4)
    g4b = 0
    for mh in range(2):
        t.wait_ge(S["p4w"], 16 * (3 + 2 * mh))
        for it in range(itout):
            for mc in range(2):
                bs = g4b % 2
                if g4b >= 2:
                    t.wait_ge(S["p4bev"], g4b - 1)
                for duc in range(16):
                    t.matmul(pp3[:, bs, :], hT[:, duc, it * 128:(it + 1) * 128],
                             W4[:, duc, mc * 512:(mc + 1) * 512],
                             start=(duc == 0), stop=(duc == 15))
                for dmc in range(16):
                    mm = t.matmul(pp3[:, 2 + bs, :],
                                  x1T[:, dmc, it * 128:(it + 1) * 128],
                                  W4[:, dmc, 1024 + mc * 512:1024 + (mc + 1) * 512],
                                  start=(dmc == 0), stop=(dmc == 15))
                mm.then_inc(S["p4bg"], 1)
                g4b += 1
    t.sem_inc(S["ph4b"], 1)



def emit_scalar(s, pl, d, tn, S, hreg):
    cfg = pl["cfg"]
    s.wait_ge(S["s_rdy"], 1)

    groups, blocks, blk_end = pl["p0"]
    pp0, QT, KT = pl["pp0"], pl["QT"], pl["KT"]
    for gi, g in enumerate(groups):
        if g["ev"] != "A":
            continue
        s.wait_ge(S["p0g"], gi + 1)
        dt_ = g["sub"]
        if g["p"] == "k":
            cp = s.copy(KT[:, dt_, g["col"]:g["col"] + g["w"]], pp0[:, gi % 4, 0:g["w"]])
        else:
            cp = s.mul(QT[:, dt_, g["col"]:g["col"] + g["w"]], pp0[:, gi % 4, 0:g["w"]], 0.125)
        cp.then_inc(S["p0evA"], 1)
    s.sem_inc(S["ph0"], 1)

    s.wait_ge(S["ph0"], 4)
    units = pl["units"]
    pairs, pair_of, zcumP = pl["zcum"]
    pz, po = pl["pz"], pl["po"]
    sig, sgn, obuf = pl["sig"], pl["sgn"], pl["obuf"]
    attT, pt = pl["attT"], pl["pt"]
    n = len(units)
    sweep_info = [(u.h, u.tl, u.idx) for u in units if u.sweep_end]
    nsweeps = len(sweep_info)

    def emit_o(sw):
        h, tl, jend = sweep_info[sw]
        sp = sw % 2
        s.wait_ge(S["s_av"], jend + 1)
        if sw >= 2:
            s.wait_ge(S["s_od0"] if sp == 0 else S["s_od1"], 16 * (sw // 2))
        s.copy(obuf[0:64, sp, :], po[0:64, sp, 0:128]).then_inc(S["s_oev"], 1)

    sw_done = 0
    for i in range(n + ALAG):
        if i < n and pair_of[i] is not None and pairs[pair_of[i]].start == i:
            p = pairs[pair_of[i]]
            pi = p.idx
            bp = pi % 4
            s.wait_ge(S["s_z"], pi + 1)
            if pi >= 4:
                s.wait_ge(S["s_mult"], pi - 3)
            pzflat = pz.ap()[:, (pi % 2) * 2:(pi % 2) * 2 + 2, :] \
                .rearrange("p a n -> p (a n)")
            s.activation(sig[:, bp, 0:p.w], pzflat[:, 0:p.w],
                         AF.Sigmoid).then_inc(S["s_sg"], 1)
            if pi >= 4:
                s.wait_ge(S["s_scan"], pi - 3)
            s.activation(sgn[:, bp, 1:1 + p.w], sig[:, bp, 0:p.w], AF.Copy,
                         scale=-1.0, bias=1.0).then_inc(S["s_act"], 1)
        je = i - TLAG
        if 0 <= je < n and je % 2 == 0:
            ue = units[je]
            s.wait_ge(S["s_tr"], je + 1)
            if je >= 2:
                s.wait_ge(S["s_av"], je - 1)
            s.copy(attT[:, je % 2, 0:ue.w], pt[:, je % 2, 0:ue.w]).then_inc(S["s_evA"], 1)
        j = i - ALAG
        if 0 <= j < n and units[j].sweep_end:
            emit_o(sw_done)
            sw_done += 1
    assert sw_done == nsweeps
    s.wait_ge(S["s_od0"], 16 * ((nsweeps + 1) // 2))
    s.wait_ge(S["s_od1"], 16 * (nsweeps // 2))
    s.sem_inc(S["ph1"], 1)

    s.wait_ge(S["s_xch"], 1)
    itout, ttok = cfg["itout"], cfg["ttok"]
    for hh in s.Switch(hreg, 2):
        for dt_ in range(16):
            src = d["oT_sh"][dt_ // 8, dt_ % 8, :, hh * ttok:(hh + 1) * ttok]
            s.dma_start(out=pl["oTa"][:, dt_, :], in_=src).then_inc(S["p3ld"], 16)
    xresb, rbuf, xcbuf = pl["xresb"], pl["rbuf"], pl["xcbuf"]
    x1f, stat, pp3 = pl["x1f"], pl["stat"], pl["pp3"]

    def xres_dma(it):
        if it >= 2:
            s.wait_ge(S["p3r"], 4 * (it - 1))
        s.dma_start(out=xresb[:, it % 2, :], in_=d["xres"][it * 128:(it + 1) * 128, :]) \
            .then_inc(S["xrld0"] if it % 2 == 0 else S["xrld1"], 16)

    xres_dma(0)
    if itout > 1:
        xres_dma(1)
    for it in range(itout):
        sma = pl["sma_state"]

        def sch(res):
            sma[0] += 1
            res.then_inc(S["sma"], 1)

        def swt():
            s.wait_ge(S["sma"], sma[0])

        s.wait_ge(S["p3lnA"], it + 1)
        sch(s.activation(rbuf[:, :], xcbuf[:, :], AF.Square, accum_out=stat[:, 1:2]))
        swt()
        sch(s.activation(stat[:, 2:3], stat[:, 1:2], AF.Ln, scale=1.0 / DM, bias=tn["epsb"][:]))
        swt()
        s.activation(stat[:, 3:4], stat[:, 2:3], AF.Exp, scale=-0.5).then_inc(S["p3lnB"], 1)
        s.wait_ge(S["p3tr"], it + 1)
        for q in range(4):
            s.copy(pl["x1T"].ap()[:, 4 * q:4 * q + 4, it * 128:(it + 1) * 128],
                   pp3.ap()[:, q, 0:512].rearrange("p (a n) -> p a n", a=4)) \
                .then_inc(S["p3te"], 1)
        s.wait_ge(S["p3gd"], it + 1)
        s.activation(stat[:, 9:10], stat[:, 8:9], AF.Exp).then_inc(S["p3gx"], 1)
        if it + 2 < itout:
            xres_dma(it + 2)
        s.wait_ge(S["p3ln"], it + 1)
        s.dma_start(out=d["x1_scr"][it * 128:(it + 1) * 128, :], in_=x1f[:, :]) \
            .then_inc(S["x1w"], 16)
    s.sem_inc(S["ph3"], 1)

    s.wait_ge(S["ph3"], 4)
    pp4, hT = pl["pp4"], pl["hT"]
    tokbs = [(c, min(512, ttok - c)) for c in range(0, ttok, 512)]
    g4 = 0
    for dupt in range(16):
        for (tc, tw) in tokbs:
            s.wait_ge(S["p4g"], g4 + 1)
            s.activation(hT[:, dupt, tc:tc + tw], pp4[:, g4 % 2, 0:tw], AF.Relu) \
                .then_inc(S["p4ev"], 1)
            g4 += 1
    s.sem_inc(S["ph4a"], 1)

    s.wait_ge(S["ph4a"], 4)
    s.wait_ge(S["x1w"], 16 * itout)
    x1h, r2b = pl["x1h"], pl["r2b"]
    q4 = 0
    for mh in range(2):
        for it in range(itout):
            par = q4 % 2
            if q4 >= 2:
                s.wait_ge(S["p4bev"], 2 * (q4 - 1))
            s.dma_start(out=x1h[:, par, :],
                        in_=d["x1_scr"][it * 128:(it + 1) * 128, mh * 1024:(mh + 1) * 1024]) \
                .then_inc(S["x1ld0"] if par == 0 else S["x1ld1"], 16)
            s.wait_ge(S["p4bev"], 2 * (q4 + 1))
            s.dma_start(out=d["r2_scr"][it * 128:(it + 1) * 128, mh * 1024:(mh + 1) * 1024],
                        in_=r2b[:, par, :]) \
                .then_inc(S["r2w0"] if par == 0 else S["r2w1"], 16)
            q4 += 1
    nq = 2 * itout
    s.wait_ge(S["r2w0"], 16 * ((nq + 1) // 2))
    s.wait_ge(S["r2w1"], 16 * (nq // 2))
    s.sem_inc(S["ph4b"], 1)

    s.wait_ge(S["ph4b"], 4)
    r2l, outb = pl["r2l"], pl["outb"]

    def r2_dma(it):
        s.dma_start(out=r2l[:, it % 2, :], in_=d["r2_scr"][it * 128:(it + 1) * 128, :]) \
            .then_inc(S["r2ld0"] if it % 2 == 0 else S["r2ld1"], 16)

    r2_dma(0)
    if itout > 1:
        r2_dma(1)
    for it in range(itout):
        sma = pl["sma_state"]

        def sch(res):
            sma[0] += 1
            res.then_inc(S["sma"], 1)

        def swt():
            s.wait_ge(S["sma"], sma[0])

        s.wait_ge(S["p5a"], it + 1)
        sch(s.activation(rbuf[:, :], xcbuf[:, :], AF.Square, accum_out=stat[:, 5:6]))
        swt()
        sch(s.activation(stat[:, 6:7], stat[:, 5:6], AF.Ln, scale=1.0 / DM, bias=tn["epsb"][:]))
        swt()
        s.activation(stat[:, 7:8], stat[:, 6:7], AF.Exp, scale=-0.5).then_inc(S["p5b"], 1)
        if it + 2 < itout:
            r2_dma(it + 2)
        s.wait_ge(S["p5ln"], it + 1)
        s.dma_start(out=d["out"][it * 128:(it + 1) * 128, :], in_=outb[:, it % 2, :]) \
            .then_inc(S["s_out0"] if it % 2 == 0 else S["s_out1"], 16)
    s.wait_ge(S["s_out0"], 16 * ((itout + 1) // 2))
    s.wait_ge(S["s_out1"], 16 * (itout // 2))



def emit_vector(v, pl, d, tn, S):
    cfg = pl["cfg"]
    v.wait_ge(S["s_rdy"], 1)

    groups, blocks, blk_end = pl["p0"]
    pp0, V = pl["pp0"], pl["V"]
    for gi, g in enumerate(groups):
        if g["ev"] != "D":
            continue
        v.wait_ge(S["p0g"], gi + 1)
        ch, dvb = g["sub"]
        tt_ = g["col"] // 128 + ch
        v.tensor_copy(V[:, tt_, dvb * 512:(dvb + 1) * 512], pp0[:, gi % 4, 0:512]) \
            .then_inc(S["p0evD"], 1)
    v.sem_inc(S["ph0"], 1)

    v.wait_ge(S["ph0"], 4)
    v.wait_ge(S["s_sgn"], 1)
    units = pl["units"]
    pairs, pair_of, zcumP = pl["zcum"]
    pz, pt = pl["pz"], pl["pt"]
    sgn, cbuf, attT = pl["sgn"], pl["cbuf"], pl["attT"]
    n = len(units)
    for i in range(n + TLAG):
        if i < n and pair_of[i] is not None and pairs[pair_of[i]].start == i:
            p = pairs[pair_of[i]]
            pi = p.idx
            bc = pi % 3
            v.wait_ge(S["s_act"], pi + 1)
            if pi >= 3:
                v.wait_ge(S["s_mult"], pi - 2)
            if p.sweep_start:
                init = tn["ones1"][:]
            else:
                init = cbuf[:, (pi - 1) % 3, 1024:1025]
                v.wait_ge(S["s_scan"], pi)
            v.tensor_tensor_scan(cbuf[:, bc, 0:p.w + 1], sgn[:, pi % 4, 0:p.w + 1],
                                 sgn[:, pi % 4, 0:p.w + 1], init,
                                 OP.mult, OP.bypass).then_inc(S["s_scan"], 1)
        j = i - TLAG
        if 0 <= j < n and j % 2 == 1:
            u = units[j]
            b2 = j % 2
            v.wait_ge(S["s_tr"], j + 1)
            if j >= 2:
                v.wait_ge(S["s_av"], j - 1)
            v.tensor_copy(attT[:, b2, 0:u.w], pt[:, b2, 0:u.w]).then_inc(S["s_evD"], 1)
    v.sem_inc(S["ph1"], 1)

    v.wait_ge(S["s_xch"], 1)
    itout = cfg["itout"]
    pp3, pg = pl["pp3"], pl["pg"]
    xresb, rbuf, xcbuf = pl["xresb"], pl["rbuf"], pl["xcbuf"]
    x1f, x1b = pl["x1f"], pl["x1b"]
    red, stat, gl, svec = pl["red"], pl["stat"], pl["gl"], pl["svec"]
    for it in range(itout):
        par = it % 2
        v.wait_ge(S["xrld0"] if par == 0 else S["xrld1"], 16 * (it // 2 + 1))
        if it >= 1:
            v.wait_ge(S["p3lnB"], it)
        for mc in range(4):
            v.wait_ge(S["p3mm"], 4 * it + mc + 1)
            v.tensor_tensor(rbuf[:, mc * 512:(mc + 1) * 512], pp3[:, mc, :],
                            xresb[:, par, mc * 512:(mc + 1) * 512], OP.add) \
                .then_inc(S["p3r"], 1)
        smv = pl["smv_state"]

        def vch(res):
            smv[0] += 1
            res.then_inc(S["smv"], 1)

        def vwt():
            v.wait_ge(S["smv"], smv[0])

        vch(v.tensor_reduce(red[:, 0:1], rbuf[:, 0:1024], AX.X, OP.add))
        vch(v.tensor_reduce(red[:, 1:2], rbuf[:, 1024:2048], AX.X, OP.add))
        vwt()
        vch(v.tensor_reduce(red[:, 2:3], red[:, 0:2], AX.X, OP.add))
        vwt()
        vch(v.tensor_scalar(stat[:, 0:1], red[:, 2:3], 1.0 / DM, None, OP.mult))
        vwt()
        v.tensor_scalar(xcbuf[:, :], rbuf[:, :], stat[:, 0:1], None, OP.subtract) \
            .then_inc(S["p3lnA"], 1)
        v.wait_ge(S["p3lnB"], it + 1)
        if it >= 1:
            v.wait_ge(S["x1w"], 16 * it)
        v.tensor_scalar(x1f[:, :], xcbuf[:, :], stat[:, 3:4], None, OP.mult)
        if it >= 1:
            v.wait_ge(S["p3tr"], it)
        v.tensor_scalar(x1b[:, :], xcbuf[:, :], stat[:, 3:4], None, OP.mult) \
            .then_inc(S["p3ln"], 1)
        v.wait_ge(S["p3g"], it + 1)
        vch(v.tensor_copy(gl[:, 0:8], pg[:, 0:8]))
        v.sem_inc(S["p3gv"], 1)
        vwt()
        vch(v.tensor_reduce(gl[:, 8:9], gl[:, 0:8], AX.X, OP.max))
        vwt()
        vch(v.tensor_scalar(gl[:, 10:18], gl[:, 0:8], gl[:, 8:9], None, OP.is_equal))
        vwt()
        vch(v.scalar_tensor_tensor(gl[:, 20:28], gl[:, 10:18], -1e9, gl[:, 0:8],
                                   OP.mult, OP.add))
        vwt()
        vch(v.tensor_reduce(gl[:, 28:29], gl[:, 20:28], AX.X, OP.max))
        vwt()
        v.tensor_tensor(stat[:, 8:9], gl[:, 28:29], gl[:, 8:9], OP.subtract) \
            .then_inc(S["p3gd"], 1)
        v.wait_ge(S["p3gx"], it + 1)
        vch(v.tensor_scalar(stat[:, 10:11], stat[:, 9:10], 1.0, None, OP.add))
        vwt()
        vch(v.reciprocal(stat[:, 11:12], stat[:, 10:11]))
        vwt()
        vch(v.tensor_tensor(gl[:, 29:30], stat[:, 9:10], stat[:, 11:12], OP.mult))
        vwt()
        v.tensor_tensor(svec[:, it:it + 1], gl[:, 29:30], stat[:, 11:12], OP.add)
    v.sem_inc(S["ph3"], 1)
    v.sem_inc(S["ph4a"], 1)

    v.wait_ge(S["ph4a"], 4)
    x1h, exb, shb, r2b = pl["x1h"], pl["exb"], pl["shb"], pl["r2b"]
    svec = pl["svec"]
    q4 = 0
    for mh in range(2):
        for it in range(itout):
            par = q4 % 2
            v.wait_ge(S["x1ld0"] if par == 0 else S["x1ld1"], 16 * (q4 // 2 + 1))
            if q4 >= 2:
                v.wait_ge(S["r2w0"] if par == 0 else S["r2w1"], 16 * (q4 // 2))
            for mc in range(2):
                v.wait_ge(S["p4bg"], q4 * 2 + mc + 1)
                bs = (q4 * 2 + mc) % 2
                v.tensor_scalar(exb[:, :], pp3[:, 2 + bs, :], svec[:, it:it + 1],
                                None, OP.mult)
                v.tensor_tensor(shb[:, :], exb[:, :], pp3[:, bs, :], OP.add)
                v.tensor_tensor(r2b[:, par, mc * 512:(mc + 1) * 512], shb[:, :],
                                x1h[:, par, mc * 512:(mc + 1) * 512], OP.add) \
                    .then_inc(S["p4bev"], 1)
            q4 += 1
    v.sem_inc(S["ph4b"], 1)

    v.wait_ge(S["ph4b"], 4)
    r2l, outb = pl["r2l"], pl["outb"]
    for it in range(itout):
        par = it % 2
        v.wait_ge(S["r2ld0"] if par == 0 else S["r2ld1"], 16 * (it // 2 + 1))
        if it >= 1:
            v.wait_ge(S["p5b"], it)
        smv = pl["smv_state"]

        def vch(res):
            smv[0] += 1
            res.then_inc(S["smv"], 1)

        def vwt():
            v.wait_ge(S["smv"], smv[0])

        vch(v.tensor_reduce(red[:, 0:1], r2l[:, par, 0:1024], AX.X, OP.add))
        vch(v.tensor_reduce(red[:, 1:2], r2l[:, par, 1024:2048], AX.X, OP.add))
        vwt()
        vch(v.tensor_reduce(red[:, 2:3], red[:, 0:2], AX.X, OP.add))
        vwt()
        vch(v.tensor_scalar(stat[:, 4:5], red[:, 2:3], 1.0 / DM, None, OP.mult))
        vwt()
        v.tensor_scalar(xcbuf[:, :], r2l[:, par, :], stat[:, 4:5], None, OP.subtract) \
            .then_inc(S["p5a"], 1)
        v.wait_ge(S["p5b"], it + 1)
        if it >= 2:
            v.wait_ge(S["s_out0"] if par == 0 else S["s_out1"], 16 * (it // 2))
        v.tensor_scalar(outb[:, par, :], xcbuf[:, :], stat[:, 7:8], None, OP.mult) \
            .then_inc(S["p5ln"], 1)



def emit_gpsimd(g, pl, d, tn, S, hreg):
    cfg = pl["cfg"]
    g.dma_start(out=tn["ident"][:], in_=d["ident"]).then_inc(S["c_ld"], 16)
    g.dma_start(out=tn["masks"][:], in_=d["masks"]).then_inc(S["c_ld"], 16)
    g.dma_start(out=tn["nonce"][:], in_=d["nonce"]).then_inc(S["c_ld"], 16)
    g.memset(tn["ones1"][:], 1.0)
    g.wait_ge(S["c_ld"], 48)
    g.memset(tn["epsb"][:], EPS).then_inc(S["s_rdy"], 1)

    groups, blocks, blk_end = pl["p0"]
    Wp, xTb = pl["Wp"], pl["xTb"]
    wsrc = {"k": d["wk"], "v": d["wv"], "q": d["wq"]}
    pass_last = {}
    for gi, gr in enumerate(groups):
        pass_last[gr["p"]] = gi + 1
    prev_pass = {"k": None, "v": "k", "q": "v"}
    seen = set()
    for (which, nblk, col, w) in blocks:
        if which not in seen:
            seen.add(which)
            pp = prev_pass[which]
            if pp is not None:
                g.wait_ge(S["p0g"], pass_last[pp])
            g.dma_start(out=Wp[:, :, :], in_=rearr(wsrc[which])).then_inc(S["p0w"], 16)
        if nblk >= 2:
            g.wait_ge(S["p0g"], blk_end[nblk - 2])
        g.dma_start(out=xTb[:, nblk % 2, :, 0:w],
                    in_=rearr(d["xT"][0:DM, col:col + w])) \
            .then_inc(S["p0x0"] if nblk % 2 == 0 else S["p0x1"], 16)
    g.sem_inc(S["ph0"], 1)

    g.wait_ge(S["ph0"], 4)
    sgn = pl["sgn"]
    g.memset(sgn[:, 0, 0:1], 1.0)
    g.memset(sgn[:, 1, 0:1], 1.0)
    g.memset(sgn[:, 2, 0:1], 1.0)
    g.memset(sgn[:, 3, 0:1], 1.0).then_inc(S["s_sgn"], 1)
    units = pl["units"]
    pairs, pair_of, zcumP = pl["zcum"]
    sig, cbuf, att, obuf = pl["sig"], pl["cbuf"], pl["att"], pl["obuf"]
    sweep_info = [(u.h, u.tl, u.idx) for u in units if u.sweep_end]

    def emit_odma(sw):
        h, tl, jend = sweep_info[sw]
        sp = sw % 2
        g.wait_ge(S["s_oev"], sw + 1)
        dst = d["oT_loc"][h // 2, (h % 2) * 64:(h % 2) * 64 + 64,
                          tl * 128:(tl + 1) * 128]
        g.dma_start(out=dst, in_=obuf[0:64, sp, :]).then_inc(
            S["s_od0"] if sp == 0 else S["s_od1"], 16)

    o_by_pair = {}
    o_tail = []
    for sw, (h, tl, jend) in enumerate(sweep_info):
        tp = pair_of[jend] + 3
        if tp < len(pairs):
            o_by_pair.setdefault(tp, []).append(sw)
        else:
            o_tail.append(sw)
    for p in pairs:
        pi = p.idx
        for sw in o_by_pair.get(pi, []):
            emit_odma(sw)
        g.wait_ge(S["s_scan"], pi + 1)
        if pi >= 3:
            g.wait_ge(S["s_tr"], pairs[pi - 3].lasthalf + 1)
        g.tensor_tensor(att[:, pi % 3, 0:p.w], sig[:, pi % 4, 0:p.w],
                        cbuf[:, pi % 3, 0:p.w], OP.mult).then_inc(S["s_mult"], 1)
    for sw in o_tail:
        emit_odma(sw)
    g.sem_inc(S["ph1"], 1)

    g.wait_ge(S["ph1"], 4)
    ut = cfg["ut"]
    with g.register(f"f0_{ut}") as f0, g.register(f"f1_{ut}") as f1, \
            g.register(f"n0_{ut}") as n0, g.register(f"n1_{ut}") as n1, \
            g.register(f"t0_{ut}") as t0, g.register(f"t1_{ut}") as t1, \
            g.register(f"cnt_{ut}") as cnt, g.register(f"cnd_{ut}") as cnd:
        g.load(n0, tn["nonce"][0:1, 0:1])
        g.load(n1, tn["nonce"][0:1, 1:2])
        qlen = cfg["qlen"]
        for hh in g.Switch(hreg, 2):
            for dt_ in range(8):
                g.dma_start(out=d["oT_sh"][hh, dt_, :, 0:qlen],
                            in_=d["oT_loc"][dt_, :, 0:qlen]).then_inc(S["s_flg"], 16)
            g.wait_ge(S["s_flg"], 16 * 8)
            g.dma_start(out=d["shflag"][hh:hh + 1, :], in_=tn["nonce"][:]) \
                .then_inc(S["s_flg"], 16)
            g.wait_ge(S["s_flg"], 16 * 9)
            peer = 1 - hh
            g.reg_mov(cnd, 1)
            g.reg_mov(cnt, 0)
            with g.While(cnd):
                g.load(f0, d["shflag"][peer:peer + 1, 0:1])
                g.load(f1, d["shflag"][peer:peer + 1, 1:2])
                g.reg_alu(t0, f0, n0, OP.is_equal)
                g.reg_alu(t1, f1, n1, OP.is_equal)
                g.reg_alu(cnd, t0, t1, OP.bitwise_and)
                g.reg_alu(cnd, cnd, 1, OP.bitwise_xor)
                g.reg_alu(cnt, cnt, 1, OP.add)
                g.reg_alu(t0, cnt, 4000000, OP.is_lt)
                g.reg_alu(cnd, cnd, t0, OP.bitwise_and)
    g.dma_start(out=pl["WoT"][:, :, :], in_=rearr(d["woT"])).then_inc(S["p3w"], 16)
    g.dma_start(out=pl["WgT"][:, :, :], in_=rearr(d["wgT"])).then_inc(S["p3w"], 16)
    g.sem_inc(S["s_xch"], 1)
    g.sem_inc(S["ph3"], 1)
    g.wait_ge(S["ph3"], 4)
    g.dma_start(out=pl["W4"][:, :, :], in_=rearr(d["wupT"])).then_inc(S["p4w"], 16)
    tokn = len(range(0, cfg["ttok"], 512))
    g.wait_ge(S["p4g"], 16 * tokn)
    g.dma_start(out=pl["W4"][:, :, 0:1024], in_=rearr(d["wdnT"][0:DM, 0:1024])) \
        .then_inc(S["p4w"], 16)
    g.dma_start(out=pl["W4"][:, :, 1024:2048], in_=rearr(d["wex"][0:DM, 0:1024])) \
        .then_inc(S["p4w"], 16)
    g.sem_inc(S["ph4a"], 1)
    itout = cfg["itout"]
    g.wait_ge(S["p4bg"], 2 * itout)
    g.dma_start(out=pl["W4"][:, :, 0:1024], in_=rearr(d["wdnT"][0:DM, 1024:2048])) \
        .then_inc(S["p4w"], 16)
    g.dma_start(out=pl["W4"][:, :, 1024:2048], in_=rearr(d["wex"][0:DM, 1024:2048])) \
        .then_inc(S["p4w"], 16)
    g.sem_inc(S["ph4b"], 1)



def build_program():
    nc = bass.Bass("TRN2", target_bir_lowering=False)
    d = {}
    d["xT"] = nc.dram_tensor("xT", [DM, 4096], F32, kind="ExternalInput").ap()
    d["xres"] = nc.dram_tensor("xres", [1408, DM], F32, kind="ExternalInput").ap()
    d["wq"] = nc.dram_tensor("wq", [DM, 1024], F32, kind="ExternalInput").ap()
    d["wk"] = nc.dram_tensor("wk", [DM, 1024], F32, kind="ExternalInput").ap()
    d["wv"] = nc.dram_tensor("wv", [DM, 1024], F32, kind="ExternalInput").ap()
    d["woT"] = nc.dram_tensor("woT", [DM, DM], F32, kind="ExternalInput").ap()
    d["wupT"] = nc.dram_tensor("wupT", [DM, DM], F32, kind="ExternalInput").ap()
    d["wdnT"] = nc.dram_tensor("wdnT", [DM, DM], F32, kind="ExternalInput").ap()
    d["wex"] = nc.dram_tensor("wex", [DM, DM], F32, kind="ExternalInput").ap()
    d["wgT"] = nc.dram_tensor("wgT", [DM, 8], F32, kind="ExternalInput").ap()
    d["masks"] = nc.dram_tensor("masks", [P, MASK_TOTAL], F32, kind="ExternalInput").ap()
    d["ident"] = nc.dram_tensor("ident", [P, P], F32, kind="ExternalInput").ap()
    d["nonce"] = nc.dram_tensor("nonce", [1, 2], U32, kind="ExternalInput").ap()
    d["out"] = nc.dram_tensor("out", [1408, DM], F32, kind="ExternalOutput").ap()
    d["oT_sh"] = nc.dram_tensor("oT_sh", [2, 8, P, 2816], BF16, addr_space="Shared").ap()
    d["shflag"] = nc.dram_tensor("shflag", [2, 2], U32, addr_space="Shared").ap()
    d["oT_loc"] = nc.dram_tensor("oT_loc", [8, P, 2816], BF16).ap()
    d["x1_scr"] = nc.dram_tensor("x1_scr", [1408, DM], F32).ap()
    d["r2_scr"] = nc.dram_tensor("r2_scr", [1408, DM], F32).ap()

    plans = []
    with ExitStack() as top:
        E = top.enter_context
        tn = {}
        tn["ident"] = E(nc.sbuf_tensor("ident_sb", [P, P], BF16))
        tn["masks"] = E(nc.sbuf_tensor("masks_sb", [P, MASK_TOTAL], BF16))
        tn["ones1"] = E(nc.sbuf_tensor("ones1", [P, 1], F32))
        tn["nonce"] = E(nc.sbuf_tensor("nonce_sb", [1, 2], U32))
        tn["epsb"] = E(nc.sbuf_tensor("epsb", [P, 1], F32))

        S = {nm: E(nc.semaphore(nm)) for nm in SEM_NAMES}

        for ut in range(2):
            cfg = body_cfg(ut)
            kvlen, qlen, ttok = cfg["kvlen"], cfg["qlen"], cfg["ttok"]
            pl = {"cfg": cfg}
            pl["units"], pl["zcum"] = make_units(cfg)
            pl["p0"] = p0_schedule(cfg)
            pl["smv_state"] = [0]
            pl["sma_state"] = [0]
            with ExitStack() as b1:
                B = b1.enter_context
                pl["QT"] = B(nc.sbuf_tensor(f"QT{ut}", [P, 8, qlen], BF16))
                pl["KT"] = B(nc.sbuf_tensor(f"KT{ut}", [P, 8, kvlen], BF16))
                pl["V"] = B(nc.sbuf_tensor(f"V{ut}", [P, kvlen // 128, 1024], BF16))
                with ExitStack() as b1a:
                    BA = b1a.enter_context
                    pl["Wp"] = BA(nc.sbuf_tensor(f"Wp{ut}", [P, 16, 1024], BF16))
                    pl["xTb"] = BA(nc.sbuf_tensor(f"xTb{ut}", [P, 2, 16, 256], BF16))
                with ExitStack() as b1b:
                    BB = b1b.enter_context
                    pl["sig"] = BB(nc.sbuf_tensor(f"sig{ut}", [P, 4, 1024], F32))
                    pl["sgn"] = BB(nc.sbuf_tensor(f"sgn{ut}", [P, 4, 1026], F32))
                    pl["cbuf"] = BB(nc.sbuf_tensor(f"cbuf{ut}", [P, 3, 1025], F32))
                    pl["att"] = BB(nc.sbuf_tensor(f"att{ut}", [P, 3, 1024], BF16))
                    pl["attT"] = BB(nc.sbuf_tensor(f"attT{ut}", [P, 2, 512], BF16))
                    pl["obuf"] = BB(nc.sbuf_tensor(f"obuf{ut}", [64, 2, 128], BF16))
            with ExitStack() as b2:
                B = b2.enter_context
                pl["x1T"] = B(nc.sbuf_tensor(f"x1T{ut}", [P, 16, ttok], BF16))
                pl["rbuf"] = B(nc.sbuf_tensor(f"rbuf{ut}", [P, DM], F32))
                pl["xcbuf"] = B(nc.sbuf_tensor(f"xcbuf{ut}", [P, DM], F32))
                pl["red"] = B(nc.sbuf_tensor(f"red{ut}", [P, 4], F32))
                pl["stat"] = B(nc.sbuf_tensor(f"stat{ut}", [P, 12], F32))
                pl["gl"] = B(nc.sbuf_tensor(f"gl{ut}", [P, 32], F32))
                pl["svec"] = B(nc.sbuf_tensor(f"svec{ut}", [P, 11], F32))
                with ExitStack() as b2a:
                    BA = b2a.enter_context
                    pl["oTa"] = BA(nc.sbuf_tensor(f"oTa{ut}", [P, 16, ttok], BF16))
                    pl["WoT"] = BA(nc.sbuf_tensor(f"WoT{ut}", [P, 16, DM], BF16))
                    pl["WgT"] = BA(nc.sbuf_tensor(f"WgT{ut}", [P, 16, 8], BF16))
                    pl["xresb"] = BA(nc.sbuf_tensor(f"xresb{ut}", [P, 2, DM], F32))
                    pl["x1f"] = BA(nc.sbuf_tensor(f"x1f{ut}", [P, DM], F32))
                    pl["x1b"] = BA(nc.sbuf_tensor(f"x1b{ut}", [P, DM], BF16))
                with ExitStack() as b2b:
                    BB = b2b.enter_context
                    pl["W4"] = BB(nc.sbuf_tensor(f"W4{ut}", [P, 16, DM], BF16))
                    pl["hT"] = BB(nc.sbuf_tensor(f"hT{ut}", [P, 16, ttok], BF16))
                    pl["x1h"] = BB(nc.sbuf_tensor(f"x1h{ut}", [P, 2, 1024], F32))
                    pl["exb"] = BB(nc.sbuf_tensor(f"exb{ut}", [P, 512], F32))
                    pl["shb"] = BB(nc.sbuf_tensor(f"shb{ut}", [P, 512], F32))
                    pl["r2b"] = BB(nc.sbuf_tensor(f"r2b{ut}", [P, 2, 1024], F32))
                with ExitStack() as b2c:
                    BC = b2c.enter_context
                    pl["r2l"] = BC(nc.sbuf_tensor(f"r2l{ut}", [P, 2, DM], F32))
                    pl["outb"] = BC(nc.sbuf_tensor(f"outb{ut}", [P, 2, DM], F32))
            with ExitStack() as ps1:
                PB = ps1.enter_context
                pl["pz"] = PB(nc.psum_tensor(f"pz{ut}", [P, DEPTH, 512], F32))
                pl["pt"] = PB(nc.psum_tensor(f"pt{ut}", [P, 2, 512], F32))
                pl["po"] = PB(nc.psum_tensor(f"po{ut}", [P, 2, 512], F32))
            with ExitStack() as ps2:
                PB = ps2.enter_context
                pl["pp0"] = PB(nc.psum_tensor(f"pp0{ut}", [P, 4, 512], F32))
            with ExitStack() as ps3:
                PB = ps3.enter_context
                pl["pp3"] = PB(nc.psum_tensor(f"pp3{ut}", [P, 4, 512], F32))
                pl["pg"] = PB(nc.psum_tensor(f"pg{ut}", [P, 512], F32))
                pl["pp4"] = PB(nc.psum_tensor(f"pp4{ut}", [P, 2, 512], F32))
            plans.append(pl)

        with nc.Block() as block:

            def wsel(e):
                return e.scalar_reg_alu(
                    OP.bitwise_and,
                    e.scalar_reg_alu(OP.arith_shift_right, e.partition_id(), 1), 1)

            @block.tensor
            def _(t):
                for w in t.Switch(wsel(t), 2):
                    emit_tensor(t, plans[w], d, tn, S)

            @block.scalar
            def _(s):
                hreg = s.scalar_reg_alu(OP.bitwise_and, s.partition_id(), 1)
                for w in s.Switch(wsel(s), 2):
                    emit_scalar(s, plans[w], d, tn, S, hreg)

            @block.vector
            def _(v):
                for w in v.Switch(wsel(v), 2):
                    emit_vector(v, plans[w], d, tn, S)

            @block.gpsimd
            def _(g):
                hreg = g.scalar_reg_alu(OP.bitwise_and, g.partition_id(), 1)
                for w in g.Switch(wsel(g), 2):
                    emit_gpsimd(g, plans[w], d, tn, S, hreg)
    return nc


_NC_CACHE = {}


def prepare(L0_attn_hidden_states, L0_attn_Wq, L0_attn_Wk, L0_attn_Wv,
            L0_attn_Wo, L0_ffn_W_up, L0_ffn_W_down, L0_ffn_W_gate,
            L0_ffn_W_expert):
    x = np.asarray(L0_attn_hidden_states)
    Wq = np.asarray(L0_attn_Wq)
    Wk = np.asarray(L0_attn_Wk)
    Wv = np.asarray(L0_attn_Wv)
    Wo = np.asarray(L0_attn_Wo)
    Wup = np.asarray(L0_ffn_W_up)
    Wdn = np.asarray(L0_ffn_W_down)
    Wg = np.asarray(L0_ffn_W_gate)
    Wex = np.asarray(L0_ffn_W_expert)

    xr = x[:, ::-1, :]
    masks = build_masks()
    ident = np.eye(P, dtype=np.float32)
    woT = np.ascontiguousarray(Wo.T)
    wupT = np.ascontiguousarray(Wup.T)
    wdnT = np.ascontiguousarray(Wdn.T)
    wgT = np.ascontiguousarray(Wg.T)
    wexc = np.ascontiguousarray(Wex)
    nonce = np.random.randint(1, 2**31, size=(1, 2)).astype(np.uint32)

    in_maps = []
    metas = []
    for c in range(8):
        beta = c // 4
        ut = (c >> 1) & 1
        half = c & 1
        cfg = body_cfg(ut)
        qlen, kvlen, ttok = cfg["qlen"], cfg["kvlen"], cfg["ttok"]
        rev0 = 0 if ut == 0 else BSPLIT
        span = max(kvlen, qlen)
        xT = np.zeros((DM, 4096), np.float32)
        xT[:, 0:span] = xr[beta, rev0:rev0 + span, :].T
        tok0 = rev0 + half * ttok
        xres = np.zeros((1408, DM), np.float32)
        xres[0:ttok] = xr[beta, tok0:tok0 + ttok, :]
        hs = half * 1024
        wq_s = np.ascontiguousarray(Wq[hs:hs + 1024, :].T)
        wk_s = np.ascontiguousarray(Wk[hs:hs + 1024, :].T)
        wv_s = np.ascontiguousarray(Wv[hs:hs + 1024, :].T)
        in_maps.append(dict(xT=xT, xres=xres, wq=wq_s, wk=wk_s, wv=wv_s,
                            woT=woT, wupT=wupT, wdnT=wdnT, wex=wexc, wgT=wgT,
                            masks=masks, ident=ident, nonce=nonce))
        metas.append((beta, tok0, ttok))
    return in_maps, metas


def assemble(core_outs, metas):
    out = np.zeros((2, LSEQ, DM), np.float32)
    for c in range(8):
        beta, tok0, ttok = metas[c]
        rows = core_outs[c][0:ttok]
        nat = LSEQ - 1 - (tok0 + np.arange(ttok))
        out[beta, nat, :] = rows
    return out


def kernel(**inputs):
    if "nc" not in _NC_CACHE:
        _NC_CACHE["nc"] = build_program()
    nc = _NC_CACHE["nc"]
    in_maps, metas = prepare(**inputs)
    res = run_bass_kernel_spmd(nc, in_maps, core_ids=list(range(8)))
    return assemble([res.results[c]["out"] for c in range(8)], metas)


# version 09
# speedup vs baseline: 2.3781x; 2.3781x over previous
import numpy as np
from contextlib import ExitStack

import concourse.bass as bass
import concourse.mybir as mybir
from concourse.bass_utils import run_bass_kernel_spmd

F32 = mybir.dt.float32
BF16 = mybir.dt.bfloat16
U32 = mybir.dt.uint32
AF = mybir.ActivationFunctionType
OP = mybir.AluOpType
AX = mybir.AxisListType
P = 128

DM = 2048
LSEQ = 4096
BSPLIT = 1280
NHL = 16
DEPTH = 4
TLAG = 7
ALAG = 9
EPS = 1e-5

MASK_OFFS = [0, 512, 1024, 1536, 2048, 2048 + 256]
MASK_TOTAL = 4 * 512 + 2 * 256

SEM_NAMES = [
    "c_ld", "s_rdy", "p0w", "p0x0", "p0x1", "p0g", "p0evA", "p0evD", "ph0",
    "s_z", "s_sg", "s_act", "s_scan", "s_mult", "s_tr", "s_ev", "s_av",
    "s_oev", "s_od0", "s_od1", "ph1", "s_flg", "s_xch", "p3ld", "p3w", "s_evA", "s_evD",
    "p3mm", "p3r", "p3lnA", "p3lnB", "p3ln", "p3tr", "p3te", "p3g", "p3gv",
    "p3gd", "p3gx", "xrld0", "xrld1", "x1w", "ph3", "p4w", "p4g", "p4ev",
    "ph4a", "p4bg", "p4bev", "x1ld0", "x1ld1", "r2w0", "r2w1", "ph4b",
    "r2ld0", "r2ld1", "p5a", "p5b", "p5ln", "s_out0", "s_out1", "s_sgn", "s_dbg", "smv", "sma",
]


class Unit:
    __slots__ = ("h", "tl", "kb", "w", "sweep", "sweep_start", "sweep_end",
                 "mvar", "zm_count", "prev_same_zpar", "idx")


def body_cfg(ut):
    if ut == 0:
        ntl, kvlen, qlen = 10, 4096, 1280

        def blocks(tl):
            return [(kb, 512) for kb in range(tl // 4, 8)]

        def mvar(tl):
            return tl % 4
    else:
        ntl, kvlen, qlen = 22, 2816, 2816

        def blocks(tl):
            kb0 = tl // 4
            bl = [(kb, 512) for kb in range(kb0, 5)]
            bl.append((5, 256))
            return bl

        def mvar(tl):
            return 4 + (tl - 20) if tl >= 20 else tl % 4
    return dict(ut=ut, ntl=ntl, kvlen=kvlen, qlen=qlen, blocks=blocks,
                mvar=mvar, ttok=qlen // 2, itout=(qlen // 2) // 128)


class Pair:
    __slots__ = ("h", "tl", "kb", "start", "nh", "w", "w0", "sweep",
                 "sweep_start", "mvar", "zm_count", "prev_same_zpar",
                 "lasthalf", "idx")


def make_units(cfg):
    units = []
    sweep = 0
    for h in range(NHL):
        for tl in range(cfg["ntl"]):
            bl = cfg["blocks"](tl)
            for bi, (kb, w) in enumerate(bl):
                u = Unit()
                u.h, u.tl, u.kb, u.w = h, tl, kb, w
                u.sweep = sweep
                u.sweep_start = bi == 0
                u.sweep_end = bi == len(bl) - 1
                u.mvar = cfg["mvar"](tl) if bi == 0 else None
                units.append(u)
            sweep += 1
    for i, u in enumerate(units):
        u.idx = i
        u.zm_count = None
    pairs = []
    pair_of = [None] * len(units)
    i = 0
    while i < len(units):
        u = units[i]
        nh = 2 if (i + 1 < len(units) and units[i + 1].sweep == u.sweep) else 1
        p = Pair()
        p.h, p.tl, p.kb = u.h, u.tl, u.kb
        p.start = i
        p.nh = nh
        p.w0 = u.w
        p.w = u.w + (units[i + 1].w if nh == 2 else 0)
        p.sweep = u.sweep
        p.sweep_start = u.sweep_start
        p.mvar = u.mvar
        p.lasthalf = i + nh - 1
        p.idx = len(pairs)
        pair_of[i] = p.idx
        if nh == 2:
            pair_of[i + 1] = p.idx
        pairs.append(p)
        i += nh
    zmc = 0
    last = {0: None, 1: None}
    zcumP = []
    for p in pairs:
        if p.mvar is not None:
            zp = zmc % 2
            p.prev_same_zpar = last[zp]
            last[zp] = p.idx
            zmc += 1
            p.zm_count = zmc
        else:
            p.zm_count = None
            p.prev_same_zpar = None
        zcumP.append(zmc)
    return units, (pairs, pair_of, zcumP)


def p0_blocks(cfg, which):
    ln = cfg["kvlen"] if which in ("k", "v") else cfg["qlen"]
    out = []
    col = 0
    while col < ln:
        w = min(256, ln - col)
        out.append((col, w))
        col += w
    return out


def p0_schedule(cfg):
    groups = []
    blocks = []
    nblk = 0
    for which in ("k", "v", "q"):
        for (col, w) in p0_blocks(cfg, which):
            blocks.append((which, nblk, col, w))
            if which == "v":
                for ch in range(w // 128):
                    for dvb in range(2):
                        groups.append(dict(p=which, nblk=nblk, col=col, w=w,
                                           sub=(ch, dvb), ev="D"))
            else:
                for dt_ in range(8):
                    groups.append(dict(p=which, nblk=nblk, col=col, w=w,
                                       sub=dt_, ev="A"))
            nblk += 1
    ca = cd = 0
    blk_end = {}
    for gi, g in enumerate(groups):
        if g["ev"] == "A":
            ca += 1
        else:
            cd += 1
        g["cumA"], g["cumD"] = ca, cd
        blk_end[g["nblk"]] = gi + 1
    widx = {"k": 1, "v": 2, "q": 3}
    for g in groups:
        g["widx"] = widx[g["p"]]
    return groups, blocks, blk_end


def build_masks():
    m = np.zeros((P, MASK_TOTAL), np.float32)
    ii = np.arange(P)[:, None]
    for q in range(4):
        jj = np.arange(512)[None, :]
        m[:, q * 512:(q + 1) * 512] = np.where(jj <= 128 * q + ii, -1e9, 0.0)
    for q in range(2):
        jj = np.arange(256)[None, :]
        off = 2048 + q * 256
        m[:, off:off + 256] = np.where(jj <= 128 * q + ii, -1e9, 0.0)
    return m


def rearr(ap):
    return ap.rearrange("(a p) n -> p a n", p=128)



def emit_tensor(t, pl, d, tn, S):
    cfg = pl["cfg"]
    t.wait_ge(S["s_rdy"], 1)

    groups, blocks, blk_end = pl["p0"]
    pp0, W, xTb = pl["pp0"], pl["Wp"], pl["xTb"]
    for gi, g in enumerate(groups):
        t.wait_ge(S["p0w"], 16 * g["widx"])
        par = g["nblk"] % 2
        t.wait_ge(S["p0x0"] if par == 0 else S["p0x1"], 16 * (g["nblk"] // 2 + 1))
        if gi >= 4:
            gp = groups[gi - 4]
            if gp["cumA"]:
                t.wait_ge(S["p0evA"], gp["cumA"])
            if gp["cumD"]:
                t.wait_ge(S["p0evD"], gp["cumD"])
        bank = gi % 4
        if g["p"] in ("k", "q"):
            dt_ = g["sub"]
            for dmc in range(16):
                mm = t.matmul(pp0[:, bank, 0:g["w"]],
                              W[:, dmc, dt_ * 128:(dt_ + 1) * 128],
                              xTb[:, par, dmc, 0:g["w"]],
                              start=(dmc == 0), stop=(dmc == 15))
        else:
            ch, dvb = g["sub"]
            for dmc in range(16):
                mm = t.matmul(pp0[:, bank, 0:512],
                              xTb[:, par, dmc, ch * 128:(ch + 1) * 128],
                              W[:, dmc, dvb * 512:(dvb + 1) * 512],
                              start=(dmc == 0), stop=(dmc == 15))
        mm.then_inc(S["p0g"], 1)
    t.sem_inc(S["ph0"], 1)

    t.wait_ge(S["ph0"], 4)
    totA = pl["p0"][0][-1]["cumA"]
    totD = pl["p0"][0][-1]["cumD"]
    t.wait_ge(S["p0evA"], totA)
    t.wait_ge(S["p0evD"], totD)
    units = pl["units"]
    pairs, pair_of, zcumP = pl["zcum"]
    QT, KT, V = pl["QT"], pl["KT"], pl["V"]
    pz, pt, po = pl["pz"], pl["pt"], pl["po"]
    att, attT = pl["att"], pl["attT"]
    n = len(units)
    for i in range(n + ALAG):
        if i < n and pair_of[i] is not None and pairs[pair_of[i]].start == i:
            p = pairs[pair_of[i]]
            pi = p.idx
            if pi >= 2:
                t.wait_ge(S["s_sg"], pi - 1)
            p0h = (p.h % 2) * 64
            for hh in range(p.nh):
                u = units[i + hh]
                qs = QT[p0h:p0h + 64, p.h // 2, p.tl * 128:(p.tl + 1) * 128]
                ks = KT[p0h:p0h + 64, p.h // 2, u.kb * 512:u.kb * 512 + u.w]
                masked = hh == 0 and p.mvar is not None
                mm = t.matmul(pz[:, (pi % 2) * 2 + hh, 0:u.w], qs, ks,
                              start=True, stop=not masked)
                if masked:
                    off = MASK_OFFS[p.mvar]
                    mm = t.matmul(pz[:, (pi % 2) * 2 + hh, 0:u.w], tn["ident"][:],
                                  tn["masks"][:, off:off + u.w],
                                  start=False, stop=True)
            mm.then_inc(S["s_z"], 1)
        j = i - TLAG
        if 0 <= j < n:
            u = units[j]
            p = pairs[pair_of[j]]
            b2 = j % 2
            t.wait_ge(S["s_mult"], p.idx + 1)
            if j >= 2:
                t.wait_ge(S["s_evA"] if j % 2 == 0 else S["s_evD"], j // 2)
            coff = (j - p.start) * 512
            for ch in range(u.w // 128):
                mm = t.matmul(pt[:, b2, ch * 128:(ch + 1) * 128],
                              att[:, p.idx % 3, coff + ch * 128:coff + (ch + 1) * 128],
                              tn["ident"][:], start=True, stop=True)
            mm.then_inc(S["s_tr"], 1)
        j = i - ALAG
        if 0 <= j < n:
            u = units[j]
            b2, sp = j % 2, units[j].sweep % 2
            t.wait_ge(S["s_evA"] if j % 2 == 0 else S["s_evD"], j // 2 + 1)
            if u.sweep_start and u.sweep >= 2:
                t.wait_ge(S["s_oev"], u.sweep - 1)
            nch = u.w // 128
            for ch in range(nch):
                tt_ = u.kb * 4 + ch
                mm = t.matmul(po[0:64, sp, 0:128],
                              V[:, tt_, u.h * 64:(u.h + 1) * 64],
                              attT[:, b2, ch * 128:(ch + 1) * 128],
                              start=(u.sweep_start and ch == 0),
                              stop=(u.sweep_end and ch == nch - 1))
            mm.then_inc(S["s_av"], 1)
    t.sem_inc(S["ph1"], 1)

    t.wait_ge(S["s_xch"], 1)
    n_units = len(pl["units"])
    npairs = len(pl["zcum"][0])
    nsweeps = pl["units"][-1].sweep + 1
    t.wait_ge(S["s_sg"], npairs)
    t.wait_ge(S["s_act"], npairs)
    t.wait_ge(S["s_scan"], npairs)
    t.wait_ge(S["s_evA"], (n_units + 1) // 2)
    t.wait_ge(S["s_evD"], n_units // 2)
    t.wait_ge(S["s_oev"], nsweeps)
    t.wait_ge(S["p3w"], 16 * 2)
    t.wait_ge(S["p3ld"], 16 * 16)
    itout = cfg["itout"]
    oTa, WoT, x1T, x1b = pl["oTa"], pl["WoT"], pl["x1T"], pl["x1b"]
    pp3, pg = pl["pp3"], pl["pg"]
    for it in range(itout):
        if it >= 1:
            t.wait_ge(S["p3te"], 4 * it)
        for mc in range(4):
            for dvc in range(16):
                mm = t.matmul(pp3[:, mc, :], oTa[:, dvc, it * 128:(it + 1) * 128],
                              WoT[:, dvc, mc * 512:(mc + 1) * 512],
                              start=(dvc == 0), stop=(dvc == 15))
            mm.then_inc(S["p3mm"], 1)
        t.wait_ge(S["p3ln"], it + 1)
        t.wait_ge(S["p3r"], 4 * (it + 1))
        for ch in range(16):
            mm = t.matmul(pp3[:, ch // 4, (ch % 4) * 128:(ch % 4 + 1) * 128],
                          x1b[:, ch * 128:(ch + 1) * 128], tn["ident"][:],
                          start=True, stop=True)
        mm.then_inc(S["p3tr"], 1)
        t.wait_ge(S["p3te"], 4 * (it + 1))
        if it >= 1:
            t.wait_ge(S["p3gv"], it)
        for dmc in range(16):
            mm = t.matmul(pg[:, 0:8], x1T[:, dmc, it * 128:(it + 1) * 128],
                          pl["WgT"][:, dmc, :], start=(dmc == 0), stop=(dmc == 15))
        mm.then_inc(S["p3g"], 1)
    t.sem_inc(S["ph3"], 1)

    t.wait_ge(S["ph3"], 4)
    t.wait_ge(S["p4w"], 16 * 1)
    pp4, hT, W4 = pl["pp4"], pl["hT"], pl["W4"]
    tokbs = [(c, min(512, cfg["ttok"] - c)) for c in range(0, cfg["ttok"], 512)]
    g4 = 0
    for dupt in range(16):
        for (tc, tw) in tokbs:
            if g4 >= 2:
                t.wait_ge(S["p4ev"], g4 - 1)
            for dmc in range(16):
                mm = t.matmul(pp4[:, g4 % 2, 0:tw],
                              W4[:, dmc, dupt * 128:(dupt + 1) * 128],
                              x1T[:, dmc, tc:tc + tw],
                              start=(dmc == 0), stop=(dmc == 15))
            mm.then_inc(S["p4g"], 1)
            g4 += 1
    t.sem_inc(S["ph4a"], 1)

    t.wait_ge(S["ph4a"], 4)
    g4b = 0
    for mh in range(2):
        t.wait_ge(S["p4w"], 16 * (3 + 2 * mh))
        for it in range(itout):
            for mc in range(2):
                bs = g4b % 2
                if g4b >= 2:
                    t.wait_ge(S["p4bev"], g4b - 1)
                for duc in range(16):
                    t.matmul(pp3[:, bs, :], hT[:, duc, it * 128:(it + 1) * 128],
                             W4[:, duc, mc * 512:(mc + 1) * 512],
                             start=(duc == 0), stop=(duc == 15))
                for dmc in range(16):
                    mm = t.matmul(pp3[:, 2 + bs, :],
                                  x1T[:, dmc, it * 128:(it + 1) * 128],
                                  W4[:, dmc, 1024 + mc * 512:1024 + (mc + 1) * 512],
                                  start=(dmc == 0), stop=(dmc == 15))
                mm.then_inc(S["p4bg"], 1)
                g4b += 1
    t.sem_inc(S["ph4b"], 1)



def emit_scalar(s, pl, d, tn, S, hreg):
    cfg = pl["cfg"]
    s.wait_ge(S["s_rdy"], 1)

    groups, blocks, blk_end = pl["p0"]
    pp0, QT, KT = pl["pp0"], pl["QT"], pl["KT"]
    for gi, g in enumerate(groups):
        if g["ev"] != "A":
            continue
        s.wait_ge(S["p0g"], gi + 1)
        dt_ = g["sub"]
        if g["p"] == "k":
            cp = s.copy(KT[:, dt_, g["col"]:g["col"] + g["w"]], pp0[:, gi % 4, 0:g["w"]])
        else:
            cp = s.mul(QT[:, dt_, g["col"]:g["col"] + g["w"]], pp0[:, gi % 4, 0:g["w"]], 0.125)
        cp.then_inc(S["p0evA"], 1)
    s.sem_inc(S["ph0"], 1)

    s.wait_ge(S["ph0"], 4)
    units = pl["units"]
    pairs, pair_of, zcumP = pl["zcum"]
    pz, po = pl["pz"], pl["po"]
    sig, sgn, obuf = pl["sig"], pl["sgn"], pl["obuf"]
    attT, pt = pl["attT"], pl["pt"]
    n = len(units)
    sweep_info = [(u.h, u.tl, u.idx) for u in units if u.sweep_end]
    nsweeps = len(sweep_info)

    def emit_o(sw):
        h, tl, jend = sweep_info[sw]
        sp = sw % 2
        s.wait_ge(S["s_av"], jend + 1)
        if sw >= 2:
            s.wait_ge(S["s_od0"] if sp == 0 else S["s_od1"], 16 * (sw // 2))
        s.copy(obuf[0:64, sp, :], po[0:64, sp, 0:128]).then_inc(S["s_oev"], 1)

    sw_done = 0
    for i in range(n + ALAG):
        if i < n and pair_of[i] is not None and pairs[pair_of[i]].start == i:
            p = pairs[pair_of[i]]
            pi = p.idx
            bp = pi % 4
            s.wait_ge(S["s_z"], pi + 1)
            if pi >= 4:
                s.wait_ge(S["s_mult"], pi - 3)
            pzflat = pz.ap()[:, (pi % 2) * 2:(pi % 2) * 2 + 2, :] \
                .rearrange("p a n -> p (a n)")
            s.activation(sig[:, bp, 0:p.w], pzflat[:, 0:p.w],
                         AF.Sigmoid).then_inc(S["s_sg"], 1)
            if pi >= 4:
                s.wait_ge(S["s_scan"], pi - 3)
            s.activation(sgn[:, bp, 1:1 + p.w], sig[:, bp, 0:p.w], AF.Copy,
                         scale=-1.0, bias=1.0).then_inc(S["s_act"], 1)
        je = i - TLAG
        if 0 <= je < n and je % 2 == 0:
            ue = units[je]
            s.wait_ge(S["s_tr"], je + 1)
            if je >= 2:
                s.wait_ge(S["s_av"], je - 1)
            s.copy(attT[:, je % 2, 0:ue.w], pt[:, je % 2, 0:ue.w]).then_inc(S["s_evA"], 1)
        j = i - ALAG
        if 0 <= j < n and units[j].sweep_end:
            emit_o(sw_done)
            sw_done += 1
    assert sw_done == nsweeps
    s.wait_ge(S["s_od0"], 16 * ((nsweeps + 1) // 2))
    s.wait_ge(S["s_od1"], 16 * (nsweeps // 2))
    s.sem_inc(S["ph1"], 1)

    s.wait_ge(S["s_xch"], 1)
    itout, ttok = cfg["itout"], cfg["ttok"]
    for hh in s.Switch(hreg, 2):
        for dt_ in range(16):
            src = d["oT_sh"][dt_ // 8, dt_ % 8, :, hh * ttok:(hh + 1) * ttok]
            s.dma_start(out=pl["oTa"][:, dt_, :], in_=src).then_inc(S["p3ld"], 16)
    xresb, rbuf, xcbuf = pl["xresb"], pl["rbuf"], pl["xcbuf"]
    x1f, stat, pp3 = pl["x1f"], pl["stat"], pl["pp3"]

    def xres_dma(it):
        if it >= 2:
            s.wait_ge(S["p3r"], 4 * (it - 1))
        s.dma_start(out=xresb[:, it % 2, :], in_=d["xres"][it * 128:(it + 1) * 128, :]) \
            .then_inc(S["xrld0"] if it % 2 == 0 else S["xrld1"], 16)

    xres_dma(0)
    if itout > 1:
        xres_dma(1)
    for it in range(itout):
        sma = pl["sma_state"]

        def sch(res):
            sma[0] += 1
            res.then_inc(S["sma"], 1)

        def swt():
            s.wait_ge(S["sma"], sma[0])

        s.wait_ge(S["p3lnA"], it + 1)
        sch(s.activation(rbuf[:, :], xcbuf[:, :], AF.Square, accum_out=stat[:, 1:2]))
        swt()
        sch(s.activation(stat[:, 2:3], stat[:, 1:2], AF.Ln, scale=1.0 / DM, bias=tn["epsb"][:]))
        swt()
        s.activation(stat[:, 3:4], stat[:, 2:3], AF.Exp, scale=-0.5).then_inc(S["p3lnB"], 1)
        s.wait_ge(S["p3tr"], it + 1)
        for q in range(4):
            s.copy(pl["x1T"].ap()[:, 4 * q:4 * q + 4, it * 128:(it + 1) * 128],
                   pp3.ap()[:, q, 0:512].rearrange("p (a n) -> p a n", a=4)) \
                .then_inc(S["p3te"], 1)
        s.wait_ge(S["p3gd"], it + 1)
        s.activation(stat[:, 9:10], stat[:, 8:9], AF.Exp).then_inc(S["p3gx"], 1)
        if it + 2 < itout:
            xres_dma(it + 2)
        s.wait_ge(S["p3ln"], it + 1)
        s.dma_start(out=d["x1_scr"][it * 128:(it + 1) * 128, :], in_=x1f[:, :]) \
            .then_inc(S["x1w"], 16)
    s.sem_inc(S["ph3"], 1)

    s.wait_ge(S["ph3"], 4)
    pp4, hT = pl["pp4"], pl["hT"]
    tokbs = [(c, min(512, ttok - c)) for c in range(0, ttok, 512)]
    g4 = 0
    for dupt in range(16):
        for (tc, tw) in tokbs:
            s.wait_ge(S["p4g"], g4 + 1)
            s.activation(hT[:, dupt, tc:tc + tw], pp4[:, g4 % 2, 0:tw], AF.Relu) \
                .then_inc(S["p4ev"], 1)
            g4 += 1
    s.sem_inc(S["ph4a"], 1)

    s.wait_ge(S["ph4a"], 4)
    s.wait_ge(S["x1w"], 16 * itout)
    x1h, r2b = pl["x1h"], pl["r2b"]
    q4 = 0
    for mh in range(2):
        for it in range(itout):
            par = q4 % 2
            if q4 >= 2:
                s.wait_ge(S["p4bev"], 2 * (q4 - 1))
            s.dma_start(out=x1h[:, par, :],
                        in_=d["x1_scr"][it * 128:(it + 1) * 128, mh * 1024:(mh + 1) * 1024]) \
                .then_inc(S["x1ld0"] if par == 0 else S["x1ld1"], 16)
            s.wait_ge(S["p4bev"], 2 * (q4 + 1))
            s.dma_start(out=d["r2_scr"][it * 128:(it + 1) * 128, mh * 1024:(mh + 1) * 1024],
                        in_=r2b[:, par, :]) \
                .then_inc(S["r2w0"] if par == 0 else S["r2w1"], 16)
            q4 += 1
    nq = 2 * itout
    s.wait_ge(S["r2w0"], 16 * ((nq + 1) // 2))
    s.wait_ge(S["r2w1"], 16 * (nq // 2))
    s.sem_inc(S["ph4b"], 1)

    s.wait_ge(S["ph4b"], 4)
    r2l, outb = pl["r2l"], pl["outb"]

    def r2_dma(it):
        s.dma_start(out=r2l[:, it % 2, :], in_=d["r2_scr"][it * 128:(it + 1) * 128, :]) \
            .then_inc(S["r2ld0"] if it % 2 == 0 else S["r2ld1"], 16)

    r2_dma(0)
    if itout > 1:
        r2_dma(1)
    for it in range(itout):
        sma = pl["sma_state"]

        def sch(res):
            sma[0] += 1
            res.then_inc(S["sma"], 1)

        def swt():
            s.wait_ge(S["sma"], sma[0])

        s.wait_ge(S["p5a"], it + 1)
        sch(s.activation(rbuf[:, :], xcbuf[:, :], AF.Square, accum_out=stat[:, 5:6]))
        swt()
        sch(s.activation(stat[:, 6:7], stat[:, 5:6], AF.Ln, scale=1.0 / DM, bias=tn["epsb"][:]))
        swt()
        s.activation(stat[:, 7:8], stat[:, 6:7], AF.Exp, scale=-0.5).then_inc(S["p5b"], 1)
        if it + 2 < itout:
            r2_dma(it + 2)
        s.wait_ge(S["p5ln"], it + 1)
        s.dma_start(out=d["out"][it * 128:(it + 1) * 128, :], in_=outb[:, it % 2, :]) \
            .then_inc(S["s_out0"] if it % 2 == 0 else S["s_out1"], 16)
    s.wait_ge(S["s_out0"], 16 * ((itout + 1) // 2))
    s.wait_ge(S["s_out1"], 16 * (itout // 2))



def emit_vector(v, pl, d, tn, S):
    cfg = pl["cfg"]
    v.wait_ge(S["s_rdy"], 1)

    groups, blocks, blk_end = pl["p0"]
    pp0, V = pl["pp0"], pl["V"]
    for gi, g in enumerate(groups):
        if g["ev"] != "D":
            continue
        v.wait_ge(S["p0g"], gi + 1)
        ch, dvb = g["sub"]
        tt_ = g["col"] // 128 + ch
        v.tensor_copy(V[:, tt_, dvb * 512:(dvb + 1) * 512], pp0[:, gi % 4, 0:512]) \
            .then_inc(S["p0evD"], 1)
    v.sem_inc(S["ph0"], 1)

    v.wait_ge(S["ph0"], 4)
    v.wait_ge(S["s_sgn"], 1)
    units = pl["units"]
    pairs, pair_of, zcumP = pl["zcum"]
    pz, pt = pl["pz"], pl["pt"]
    sgn, cbuf, attT = pl["sgn"], pl["cbuf"], pl["attT"]
    n = len(units)
    for i in range(n + TLAG):
        if i < n and pair_of[i] is not None and pairs[pair_of[i]].start == i:
            p = pairs[pair_of[i]]
            pi = p.idx
            bc = pi % 3
            v.wait_ge(S["s_act"], pi + 1)
            if pi >= 3:
                v.wait_ge(S["s_mult"], pi - 2)
            if p.sweep_start:
                init = tn["ones1"][:]
            else:
                init = cbuf[:, (pi - 1) % 3, 1024:1025]
                v.wait_ge(S["s_scan"], pi)
            v.tensor_tensor_scan(cbuf[:, bc, 0:p.w + 1], sgn[:, pi % 4, 0:p.w + 1],
                                 sgn[:, pi % 4, 0:p.w + 1], init,
                                 OP.mult, OP.bypass).then_inc(S["s_scan"], 1)
        j = i - TLAG
        if 0 <= j < n and j % 2 == 1:
            u = units[j]
            b2 = j % 2
            v.wait_ge(S["s_tr"], j + 1)
            if j >= 2:
                v.wait_ge(S["s_av"], j - 1)
            v.tensor_copy(attT[:, b2, 0:u.w], pt[:, b2, 0:u.w]).then_inc(S["s_evD"], 1)
    v.sem_inc(S["ph1"], 1)

    v.wait_ge(S["s_xch"], 1)
    itout = cfg["itout"]
    pp3, pg = pl["pp3"], pl["pg"]
    xresb, rbuf, xcbuf = pl["xresb"], pl["rbuf"], pl["xcbuf"]
    x1f, x1b = pl["x1f"], pl["x1b"]
    red, stat, gl, svec = pl["red"], pl["stat"], pl["gl"], pl["svec"]
    for it in range(itout):
        par = it % 2
        v.wait_ge(S["xrld0"] if par == 0 else S["xrld1"], 16 * (it // 2 + 1))
        if it >= 1:
            v.wait_ge(S["p3lnB"], it)
        for mc in range(4):
            v.wait_ge(S["p3mm"], 4 * it + mc + 1)
            v.tensor_tensor(rbuf[:, mc * 512:(mc + 1) * 512], pp3[:, mc, :],
                            xresb[:, par, mc * 512:(mc + 1) * 512], OP.add) \
                .then_inc(S["p3r"], 1)
        smv = pl["smv_state"]

        def vch(res):
            smv[0] += 1
            res.then_inc(S["smv"], 1)

        def vwt():
            v.wait_ge(S["smv"], smv[0])

        vch(v.tensor_reduce(red[:, 0:1], rbuf[:, 0:1024], AX.X, OP.add))
        vch(v.tensor_reduce(red[:, 1:2], rbuf[:, 1024:2048], AX.X, OP.add))
        vwt()
        vch(v.tensor_reduce(red[:, 2:3], red[:, 0:2], AX.X, OP.add))
        vwt()
        vch(v.tensor_scalar(stat[:, 0:1], red[:, 2:3], 1.0 / DM, None, OP.mult))
        vwt()
        v.tensor_scalar(xcbuf[:, :], rbuf[:, :], stat[:, 0:1], None, OP.subtract) \
            .then_inc(S["p3lnA"], 1)
        v.wait_ge(S["p3lnB"], it + 1)
        if it >= 1:
            v.wait_ge(S["x1w"], 16 * it)
        v.tensor_scalar(x1f[:, :], xcbuf[:, :], stat[:, 3:4], None, OP.mult)
        if it >= 1:
            v.wait_ge(S["p3tr"], it)
        v.tensor_scalar(x1b[:, :], xcbuf[:, :], stat[:, 3:4], None, OP.mult) \
            .then_inc(S["p3ln"], 1)
        v.wait_ge(S["p3g"], it + 1)
        vch(v.tensor_copy(gl[:, 0:8], pg[:, 0:8]))
        v.sem_inc(S["p3gv"], 1)
        vwt()
        vch(v.tensor_reduce(gl[:, 8:9], gl[:, 0:8], AX.X, OP.max))
        vwt()
        vch(v.tensor_scalar(gl[:, 10:18], gl[:, 0:8], gl[:, 8:9], None, OP.is_equal))
        vwt()
        vch(v.scalar_tensor_tensor(gl[:, 20:28], gl[:, 10:18], -1e9, gl[:, 0:8],
                                   OP.mult, OP.add))
        vwt()
        vch(v.tensor_reduce(gl[:, 28:29], gl[:, 20:28], AX.X, OP.max))
        vwt()
        v.tensor_tensor(stat[:, 8:9], gl[:, 28:29], gl[:, 8:9], OP.subtract) \
            .then_inc(S["p3gd"], 1)
        v.wait_ge(S["p3gx"], it + 1)
        vch(v.tensor_scalar(stat[:, 10:11], stat[:, 9:10], 1.0, None, OP.add))
        vwt()
        vch(v.reciprocal(stat[:, 11:12], stat[:, 10:11]))
        vwt()
        vch(v.tensor_tensor(gl[:, 29:30], stat[:, 9:10], stat[:, 11:12], OP.mult))
        vwt()
        v.tensor_tensor(svec[:, it:it + 1], gl[:, 29:30], stat[:, 11:12], OP.add)
    v.sem_inc(S["ph3"], 1)
    v.sem_inc(S["ph4a"], 1)

    v.wait_ge(S["ph4a"], 4)
    x1h, exb, shb, r2b = pl["x1h"], pl["exb"], pl["shb"], pl["r2b"]
    svec = pl["svec"]
    q4 = 0
    for mh in range(2):
        for it in range(itout):
            par = q4 % 2
            v.wait_ge(S["x1ld0"] if par == 0 else S["x1ld1"], 16 * (q4 // 2 + 1))
            if q4 >= 2:
                v.wait_ge(S["r2w0"] if par == 0 else S["r2w1"], 16 * (q4 // 2))
            for mc in range(2):
                v.wait_ge(S["p4bg"], q4 * 2 + mc + 1)
                bs = (q4 * 2 + mc) % 2
                v.tensor_scalar(exb[:, :], pp3[:, 2 + bs, :], svec[:, it:it + 1],
                                None, OP.mult)
                v.tensor_tensor(shb[:, :], exb[:, :], pp3[:, bs, :], OP.add)
                v.tensor_tensor(r2b[:, par, mc * 512:(mc + 1) * 512], shb[:, :],
                                x1h[:, par, mc * 512:(mc + 1) * 512], OP.add) \
                    .then_inc(S["p4bev"], 1)
            q4 += 1
    v.sem_inc(S["ph4b"], 1)

    v.wait_ge(S["ph4b"], 4)
    r2l, outb = pl["r2l"], pl["outb"]
    for it in range(itout):
        par = it % 2
        v.wait_ge(S["r2ld0"] if par == 0 else S["r2ld1"], 16 * (it // 2 + 1))
        if it >= 1:
            v.wait_ge(S["p5b"], it)
        smv = pl["smv_state"]

        def vch(res):
            smv[0] += 1
            res.then_inc(S["smv"], 1)

        def vwt():
            v.wait_ge(S["smv"], smv[0])

        vch(v.tensor_reduce(red[:, 0:1], r2l[:, par, 0:1024], AX.X, OP.add))
        vch(v.tensor_reduce(red[:, 1:2], r2l[:, par, 1024:2048], AX.X, OP.add))
        vwt()
        vch(v.tensor_reduce(red[:, 2:3], red[:, 0:2], AX.X, OP.add))
        vwt()
        vch(v.tensor_scalar(stat[:, 4:5], red[:, 2:3], 1.0 / DM, None, OP.mult))
        vwt()
        v.tensor_scalar(xcbuf[:, :], r2l[:, par, :], stat[:, 4:5], None, OP.subtract) \
            .then_inc(S["p5a"], 1)
        v.wait_ge(S["p5b"], it + 1)
        if it >= 2:
            v.wait_ge(S["s_out0"] if par == 0 else S["s_out1"], 16 * (it // 2))
        v.tensor_scalar(outb[:, par, :], xcbuf[:, :], stat[:, 7:8], None, OP.mult) \
            .then_inc(S["p5ln"], 1)



def emit_gpsimd(g, pl, d, tn, S, hreg):
    cfg = pl["cfg"]
    g.dma_start(out=tn["ident"][:], in_=d["ident"]).then_inc(S["c_ld"], 16)
    g.dma_start(out=tn["masks"][:], in_=d["masks"]).then_inc(S["c_ld"], 16)
    g.dma_start(out=tn["nonce"][:], in_=d["nonce"]).then_inc(S["c_ld"], 16)
    g.memset(tn["ones1"][:], 1.0)
    g.wait_ge(S["c_ld"], 48)
    g.memset(tn["epsb"][:], EPS).then_inc(S["s_rdy"], 1)

    groups, blocks, blk_end = pl["p0"]
    Wp, xTb = pl["Wp"], pl["xTb"]
    wsrc = {"k": d["wk"], "v": d["wv"], "q": d["wq"]}
    pass_last = {}
    for gi, gr in enumerate(groups):
        pass_last[gr["p"]] = gi + 1
    prev_pass = {"k": None, "v": "k", "q": "v"}
    seen = set()
    for (which, nblk, col, w) in blocks:
        if which not in seen:
            seen.add(which)
            pp = prev_pass[which]
            if pp is not None:
                g.wait_ge(S["p0g"], pass_last[pp])
            g.dma_start(out=Wp[:, :, :], in_=rearr(wsrc[which])).then_inc(S["p0w"], 16)
        if nblk >= 2:
            g.wait_ge(S["p0g"], blk_end[nblk - 2])
        g.dma_start(out=xTb[:, nblk % 2, :, 0:w],
                    in_=rearr(d["xT"][0:DM, col:col + w])) \
            .then_inc(S["p0x0"] if nblk % 2 == 0 else S["p0x1"], 16)
    g.sem_inc(S["ph0"], 1)

    g.wait_ge(S["ph0"], 4)
    sgn = pl["sgn"]
    g.memset(sgn[:, 0, 0:1], 1.0)
    g.memset(sgn[:, 1, 0:1], 1.0)
    g.memset(sgn[:, 2, 0:1], 1.0)
    g.memset(sgn[:, 3, 0:1], 1.0).then_inc(S["s_sgn"], 1)
    units = pl["units"]
    pairs, pair_of, zcumP = pl["zcum"]
    sig, cbuf, att, obuf = pl["sig"], pl["cbuf"], pl["att"], pl["obuf"]
    sweep_info = [(u.h, u.tl, u.idx) for u in units if u.sweep_end]

    def emit_odma(sw):
        h, tl, jend = sweep_info[sw]
        sp = sw % 2
        g.wait_ge(S["s_oev"], sw + 1)
        dst = d["oT_loc"][h // 2, (h % 2) * 64:(h % 2) * 64 + 64,
                          tl * 128:(tl + 1) * 128]
        g.dma_start(out=dst, in_=obuf[0:64, sp, :]).then_inc(
            S["s_od0"] if sp == 0 else S["s_od1"], 16)

    o_by_pair = {}
    o_tail = []
    for sw, (h, tl, jend) in enumerate(sweep_info):
        tp = pair_of[jend] + 3
        if tp < len(pairs):
            o_by_pair.setdefault(tp, []).append(sw)
        else:
            o_tail.append(sw)
    for p in pairs:
        pi = p.idx
        for sw in o_by_pair.get(pi, []):
            emit_odma(sw)
        g.wait_ge(S["s_scan"], pi + 1)
        if pi >= 3:
            g.wait_ge(S["s_tr"], pairs[pi - 3].lasthalf + 1)
        g.tensor_tensor(att[:, pi % 3, 0:p.w], sig[:, pi % 4, 0:p.w],
                        cbuf[:, pi % 3, 0:p.w], OP.mult).then_inc(S["s_mult"], 1)
    for sw in o_tail:
        emit_odma(sw)
    g.sem_inc(S["ph1"], 1)

    g.wait_ge(S["ph1"], 4)
    ut = cfg["ut"]
    with g.register(f"f0_{ut}") as f0, g.register(f"f1_{ut}") as f1, \
            g.register(f"n0_{ut}") as n0, g.register(f"n1_{ut}") as n1, \
            g.register(f"t0_{ut}") as t0, g.register(f"t1_{ut}") as t1, \
            g.register(f"cnt_{ut}") as cnt, g.register(f"cnd_{ut}") as cnd:
        g.load(n0, tn["nonce"][0:1, 0:1])
        g.load(n1, tn["nonce"][0:1, 1:2])
        qlen = cfg["qlen"]
        for hh in g.Switch(hreg, 2):
            for dt_ in range(8):
                g.dma_start(out=d["oT_sh"][hh, dt_, :, 0:qlen],
                            in_=d["oT_loc"][dt_, :, 0:qlen]).then_inc(S["s_flg"], 16)
            g.wait_ge(S["s_flg"], 16 * 8)
            g.dma_start(out=d["shflag"][hh:hh + 1, :], in_=tn["nonce"][:]) \
                .then_inc(S["s_flg"], 16)
            g.wait_ge(S["s_flg"], 16 * 9)
            peer = 1 - hh
            g.reg_mov(cnd, 1)
            g.reg_mov(cnt, 0)
            with g.While(cnd):
                g.dma_start(out=tn["flagb"][:],
                            in_=d["shflag"][peer:peer + 1, :]).then_inc(S["s_dbg"], 16)
                g.load(f0, tn["flagb"][0:1, 0:1])
                g.load(f1, tn["flagb"][0:1, 1:2])
                g.reg_alu(t0, f0, n0, OP.is_equal)
                g.reg_alu(t1, f1, n1, OP.is_equal)
                g.reg_alu(cnd, t0, t1, OP.bitwise_and)
                g.reg_alu(cnd, cnd, 1, OP.bitwise_xor)
                g.reg_alu(cnt, cnt, 1, OP.add)
                g.reg_alu(t0, cnt, 4000000, OP.is_lt)
                g.reg_alu(cnd, cnd, t0, OP.bitwise_and)
    g.dma_start(out=pl["WoT"][:, :, :], in_=rearr(d["woT"])).then_inc(S["p3w"], 16)
    g.dma_start(out=pl["WgT"][:, :, :], in_=rearr(d["wgT"])).then_inc(S["p3w"], 16)
    g.sem_inc(S["s_xch"], 1)
    g.sem_inc(S["ph3"], 1)
    g.wait_ge(S["ph3"], 4)
    g.dma_start(out=pl["W4"][:, :, :], in_=rearr(d["wupT"])).then_inc(S["p4w"], 16)
    tokn = len(range(0, cfg["ttok"], 512))
    g.wait_ge(S["p4g"], 16 * tokn)
    g.dma_start(out=pl["W4"][:, :, 0:1024], in_=rearr(d["wdnT"][0:DM, 0:1024])) \
        .then_inc(S["p4w"], 16)
    g.dma_start(out=pl["W4"][:, :, 1024:2048], in_=rearr(d["wex"][0:DM, 0:1024])) \
        .then_inc(S["p4w"], 16)
    g.sem_inc(S["ph4a"], 1)
    itout = cfg["itout"]
    g.wait_ge(S["p4bg"], 2 * itout)
    g.dma_start(out=pl["W4"][:, :, 0:1024], in_=rearr(d["wdnT"][0:DM, 1024:2048])) \
        .then_inc(S["p4w"], 16)
    g.dma_start(out=pl["W4"][:, :, 1024:2048], in_=rearr(d["wex"][0:DM, 1024:2048])) \
        .then_inc(S["p4w"], 16)
    g.sem_inc(S["ph4b"], 1)



def build_program():
    nc = bass.Bass("TRN2", target_bir_lowering=False)
    d = {}
    d["xT"] = nc.dram_tensor("xT", [DM, 4096], F32, kind="ExternalInput").ap()
    d["xres"] = nc.dram_tensor("xres", [1408, DM], F32, kind="ExternalInput").ap()
    d["wq"] = nc.dram_tensor("wq", [DM, 1024], F32, kind="ExternalInput").ap()
    d["wk"] = nc.dram_tensor("wk", [DM, 1024], F32, kind="ExternalInput").ap()
    d["wv"] = nc.dram_tensor("wv", [DM, 1024], F32, kind="ExternalInput").ap()
    d["woT"] = nc.dram_tensor("woT", [DM, DM], F32, kind="ExternalInput").ap()
    d["wupT"] = nc.dram_tensor("wupT", [DM, DM], F32, kind="ExternalInput").ap()
    d["wdnT"] = nc.dram_tensor("wdnT", [DM, DM], F32, kind="ExternalInput").ap()
    d["wex"] = nc.dram_tensor("wex", [DM, DM], F32, kind="ExternalInput").ap()
    d["wgT"] = nc.dram_tensor("wgT", [DM, 8], F32, kind="ExternalInput").ap()
    d["masks"] = nc.dram_tensor("masks", [P, MASK_TOTAL], F32, kind="ExternalInput").ap()
    d["ident"] = nc.dram_tensor("ident", [P, P], F32, kind="ExternalInput").ap()
    d["nonce"] = nc.dram_tensor("nonce", [1, 2], U32, kind="ExternalInput").ap()
    d["out"] = nc.dram_tensor("out", [1408, DM], F32, kind="ExternalOutput").ap()
    d["oT_sh"] = nc.dram_tensor("oT_sh", [2, 8, P, 2816], BF16, addr_space="Shared").ap()
    d["shflag"] = nc.dram_tensor("shflag", [2, 2], U32, addr_space="Shared").ap()
    d["oT_loc"] = nc.dram_tensor("oT_loc", [8, P, 2816], BF16).ap()
    d["x1_scr"] = nc.dram_tensor("x1_scr", [1408, DM], F32).ap()
    d["r2_scr"] = nc.dram_tensor("r2_scr", [1408, DM], F32).ap()

    plans = []
    with ExitStack() as top:
        E = top.enter_context
        tn = {}
        tn["ident"] = E(nc.sbuf_tensor("ident_sb", [P, P], BF16))
        tn["masks"] = E(nc.sbuf_tensor("masks_sb", [P, MASK_TOTAL], BF16))
        tn["ones1"] = E(nc.sbuf_tensor("ones1", [P, 1], F32))
        tn["nonce"] = E(nc.sbuf_tensor("nonce_sb", [1, 2], U32))
        tn["flagb"] = E(nc.sbuf_tensor("flagb_sb", [1, 2], U32))
        tn["epsb"] = E(nc.sbuf_tensor("epsb", [P, 1], F32))

        S = {nm: E(nc.semaphore(nm)) for nm in SEM_NAMES}

        for ut in range(2):
            cfg = body_cfg(ut)
            kvlen, qlen, ttok = cfg["kvlen"], cfg["qlen"], cfg["ttok"]
            pl = {"cfg": cfg}
            pl["units"], pl["zcum"] = make_units(cfg)
            pl["p0"] = p0_schedule(cfg)
            pl["smv_state"] = [0]
            pl["sma_state"] = [0]
            with ExitStack() as b1:
                B = b1.enter_context
                pl["QT"] = B(nc.sbuf_tensor(f"QT{ut}", [P, 8, qlen], BF16))
                pl["KT"] = B(nc.sbuf_tensor(f"KT{ut}", [P, 8, kvlen], BF16))
                pl["V"] = B(nc.sbuf_tensor(f"V{ut}", [P, kvlen // 128, 1024], BF16))
                with ExitStack() as b1a:
                    BA = b1a.enter_context
                    pl["Wp"] = BA(nc.sbuf_tensor(f"Wp{ut}", [P, 16, 1024], BF16))
                    pl["xTb"] = BA(nc.sbuf_tensor(f"xTb{ut}", [P, 2, 16, 256], BF16))
                with ExitStack() as b1b:
                    BB = b1b.enter_context
                    pl["sig"] = BB(nc.sbuf_tensor(f"sig{ut}", [P, 4, 1024], F32))
                    pl["sgn"] = BB(nc.sbuf_tensor(f"sgn{ut}", [P, 4, 1026], F32))
                    pl["cbuf"] = BB(nc.sbuf_tensor(f"cbuf{ut}", [P, 3, 1025], F32))
                    pl["att"] = BB(nc.sbuf_tensor(f"att{ut}", [P, 3, 1024], BF16))
                    pl["attT"] = BB(nc.sbuf_tensor(f"attT{ut}", [P, 2, 512], BF16))
                    pl["obuf"] = BB(nc.sbuf_tensor(f"obuf{ut}", [64, 2, 128], BF16))
            with ExitStack() as b2:
                B = b2.enter_context
                pl["x1T"] = B(nc.sbuf_tensor(f"x1T{ut}", [P, 16, ttok], BF16))
                pl["rbuf"] = B(nc.sbuf_tensor(f"rbuf{ut}", [P, DM], F32))
                pl["xcbuf"] = B(nc.sbuf_tensor(f"xcbuf{ut}", [P, DM], F32))
                pl["red"] = B(nc.sbuf_tensor(f"red{ut}", [P, 4], F32))
                pl["stat"] = B(nc.sbuf_tensor(f"stat{ut}", [P, 12], F32))
                pl["gl"] = B(nc.sbuf_tensor(f"gl{ut}", [P, 32], F32))
                pl["svec"] = B(nc.sbuf_tensor(f"svec{ut}", [P, 11], F32))
                with ExitStack() as b2a:
                    BA = b2a.enter_context
                    pl["oTa"] = BA(nc.sbuf_tensor(f"oTa{ut}", [P, 16, ttok], BF16))
                    pl["WoT"] = BA(nc.sbuf_tensor(f"WoT{ut}", [P, 16, DM], BF16))
                    pl["WgT"] = BA(nc.sbuf_tensor(f"WgT{ut}", [P, 16, 8], BF16))
                    pl["xresb"] = BA(nc.sbuf_tensor(f"xresb{ut}", [P, 2, DM], F32))
                    pl["x1f"] = BA(nc.sbuf_tensor(f"x1f{ut}", [P, DM], F32))
                    pl["x1b"] = BA(nc.sbuf_tensor(f"x1b{ut}", [P, DM], BF16))
                with ExitStack() as b2b:
                    BB = b2b.enter_context
                    pl["W4"] = BB(nc.sbuf_tensor(f"W4{ut}", [P, 16, DM], BF16))
                    pl["hT"] = BB(nc.sbuf_tensor(f"hT{ut}", [P, 16, ttok], BF16))
                    pl["x1h"] = BB(nc.sbuf_tensor(f"x1h{ut}", [P, 2, 1024], F32))
                    pl["exb"] = BB(nc.sbuf_tensor(f"exb{ut}", [P, 512], F32))
                    pl["shb"] = BB(nc.sbuf_tensor(f"shb{ut}", [P, 512], F32))
                    pl["r2b"] = BB(nc.sbuf_tensor(f"r2b{ut}", [P, 2, 1024], F32))
                with ExitStack() as b2c:
                    BC = b2c.enter_context
                    pl["r2l"] = BC(nc.sbuf_tensor(f"r2l{ut}", [P, 2, DM], F32))
                    pl["outb"] = BC(nc.sbuf_tensor(f"outb{ut}", [P, 2, DM], F32))
            with ExitStack() as ps1:
                PB = ps1.enter_context
                pl["pz"] = PB(nc.psum_tensor(f"pz{ut}", [P, DEPTH, 512], F32))
                pl["pt"] = PB(nc.psum_tensor(f"pt{ut}", [P, 2, 512], F32))
                pl["po"] = PB(nc.psum_tensor(f"po{ut}", [P, 2, 512], F32))
            with ExitStack() as ps2:
                PB = ps2.enter_context
                pl["pp0"] = PB(nc.psum_tensor(f"pp0{ut}", [P, 4, 512], F32))
            with ExitStack() as ps3:
                PB = ps3.enter_context
                pl["pp3"] = PB(nc.psum_tensor(f"pp3{ut}", [P, 4, 512], F32))
                pl["pg"] = PB(nc.psum_tensor(f"pg{ut}", [P, 512], F32))
                pl["pp4"] = PB(nc.psum_tensor(f"pp4{ut}", [P, 2, 512], F32))
            plans.append(pl)

        with nc.Block() as block:

            def wsel(e):
                return e.scalar_reg_alu(
                    OP.bitwise_and,
                    e.scalar_reg_alu(OP.arith_shift_right, e.partition_id(), 1), 1)

            @block.tensor
            def _(t):
                for w in t.Switch(wsel(t), 2):
                    emit_tensor(t, plans[w], d, tn, S)

            @block.scalar
            def _(s):
                hreg = s.scalar_reg_alu(OP.bitwise_and, s.partition_id(), 1)
                for w in s.Switch(wsel(s), 2):
                    emit_scalar(s, plans[w], d, tn, S, hreg)

            @block.vector
            def _(v):
                for w in v.Switch(wsel(v), 2):
                    emit_vector(v, plans[w], d, tn, S)

            @block.gpsimd
            def _(g):
                hreg = g.scalar_reg_alu(OP.bitwise_and, g.partition_id(), 1)
                for w in g.Switch(wsel(g), 2):
                    emit_gpsimd(g, plans[w], d, tn, S, hreg)
    return nc


_NC_CACHE = {}


def prepare(L0_attn_hidden_states, L0_attn_Wq, L0_attn_Wk, L0_attn_Wv,
            L0_attn_Wo, L0_ffn_W_up, L0_ffn_W_down, L0_ffn_W_gate,
            L0_ffn_W_expert):
    x = np.asarray(L0_attn_hidden_states)
    Wq = np.asarray(L0_attn_Wq)
    Wk = np.asarray(L0_attn_Wk)
    Wv = np.asarray(L0_attn_Wv)
    Wo = np.asarray(L0_attn_Wo)
    Wup = np.asarray(L0_ffn_W_up)
    Wdn = np.asarray(L0_ffn_W_down)
    Wg = np.asarray(L0_ffn_W_gate)
    Wex = np.asarray(L0_ffn_W_expert)

    xr = x[:, ::-1, :]
    masks = build_masks()
    ident = np.eye(P, dtype=np.float32)
    woT = np.ascontiguousarray(Wo.T)
    wupT = np.ascontiguousarray(Wup.T)
    wdnT = np.ascontiguousarray(Wdn.T)
    wgT = np.ascontiguousarray(Wg.T)
    wexc = np.ascontiguousarray(Wex)
    nonce = np.random.randint(1, 2**31, size=(1, 2)).astype(np.uint32)

    in_maps = []
    metas = []
    for c in range(8):
        beta = c // 4
        ut = (c >> 1) & 1
        half = c & 1
        cfg = body_cfg(ut)
        qlen, kvlen, ttok = cfg["qlen"], cfg["kvlen"], cfg["ttok"]
        rev0 = 0 if ut == 0 else BSPLIT
        span = max(kvlen, qlen)
        xT = np.zeros((DM, 4096), np.float32)
        xT[:, 0:span] = xr[beta, rev0:rev0 + span, :].T
        tok0 = rev0 + half * ttok
        xres = np.zeros((1408, DM), np.float32)
        xres[0:ttok] = xr[beta, tok0:tok0 + ttok, :]
        hs = half * 1024
        wq_s = np.ascontiguousarray(Wq[hs:hs + 1024, :].T)
        wk_s = np.ascontiguousarray(Wk[hs:hs + 1024, :].T)
        wv_s = np.ascontiguousarray(Wv[hs:hs + 1024, :].T)
        in_maps.append(dict(xT=xT, xres=xres, wq=wq_s, wk=wk_s, wv=wv_s,
                            woT=woT, wupT=wupT, wdnT=wdnT, wex=wexc, wgT=wgT,
                            masks=masks, ident=ident, nonce=nonce))
        metas.append((beta, tok0, ttok))
    return in_maps, metas


def assemble(core_outs, metas):
    out = np.zeros((2, LSEQ, DM), np.float32)
    for c in range(8):
        beta, tok0, ttok = metas[c]
        rows = core_outs[c][0:ttok]
        nat = LSEQ - 1 - (tok0 + np.arange(ttok))
        out[beta, nat, :] = rows
    return out


def kernel(**inputs):
    if "nc" not in _NC_CACHE:
        _NC_CACHE["nc"] = build_program()
    nc = _NC_CACHE["nc"]
    in_maps, metas = prepare(**inputs)
    res = run_bass_kernel_spmd(nc, in_maps, core_ids=list(range(8)))
    return assemble([res.results[c]["out"] for c in range(8)], metas)
